# version 48
# baseline (speedup 1.0000x reference)
import numpy as np
from collections import deque
import concourse.bass as bass
import concourse.mybir as mybir
from concourse.bass_utils import run_bass_kernel_spmd
from contextlib import ExitStack

F32 = mybir.dt.float32
BF16 = mybir.dt.bfloat16
U8 = mybir.dt.uint8
I8 = mybir.dt.int8
AF = mybir.ActivationFunctionType
ALU = mybir.AluOpType
AX = mybir.AxisListType

H = 2048
S = 2048
NHEAD = 16
NHT = 16
NFT = 44
SCALE = float(1.0 / np.sqrt(128.0))
EPS = 1e-5
PR_SLOTS = 7


class Sem:
    def __init__(self, nc, name):
        self.h = nc.alloc_semaphore(name=name)
        self.count = 0
        self.name = name


COMPUTE_ENGS = ("tensor", "scalar", "vector")


class Sched:

    def __init__(self):
        self.items = []
        self.floor = {}
        self.einc = {}

    def add(self, engine, emit_fn, waits=(), inc=None):
        fl = self.floor.setdefault(engine, {})
        ei = self.einc.setdefault(engine, {})
        wd = {}
        for sem, val in waits:
            if val is None or val <= 0:
                continue
            assert val <= sem.count, (
                f"wait {sem.name}>={val} but only {sem.count} scheduled")
            if engine in COMPUTE_ENGS and ei.get(sem, 0) >= val:
                continue
            if fl.get(sem, 0) >= val:
                continue
            wd[sem] = max(wd.get(sem, 0), val)
        for sem, val in wd.items():
            fl[sem] = val
        n_inc = None
        if inc is not None:
            sem, n = inc
            sem.count += n
            n_inc = (sem, n)
            ei[sem] = ei.get(sem, 0) + n
        self.items.append((engine, wd, emit_fn, n_inc))
        return inc[0].count if inc else None

    def emit(self, name, eng):
        for engine, wd, fn, n_inc in self.items:
            if engine != name:
                continue
            for sem, val in wd.items():
                eng.wait_ge(sem.h, val)
            if fn is None:
                continue
            inst = fn(eng)
            if n_inc is not None:
                inst.then_inc(n_inc[0].h, n_inc[1])


def build_nc():
    nc = bass.Bass("TRN2", target_bir_lowering=False)

    xT = nc.dram_tensor("xT", [H, S], F32, kind="ExternalInput")
    wqk = nc.dram_tensor("wqk", [32, 128, NHT * 128], F32, kind="ExternalInput")
    wv4 = nc.dram_tensor("wv4", [4, 128, NHT * 512], F32, kind="ExternalInput")
    wo = nc.dram_tensor("wo", [16, 128, NHT * 128], F32, kind="ExternalInput")
    wg = nc.dram_tensor("wg", [NFT, 128, 16 * 128], I8, kind="ExternalInput")
    wu = nc.dram_tensor("wu", [NFT, 128, 16 * 128], I8, kind="ExternalInput")
    wdn = nc.dram_tensor("wdn", [16, 128, NFT * 128], I8, kind="ExternalInput")
    sg = nc.dram_tensor("sg", [128, NFT], F32, kind="ExternalInput")
    su = nc.dram_tensor("su", [128, NFT], F32, kind="ExternalInput")
    sd = nc.dram_tensor("sd", [128, 16], F32, kind="ExternalInput")
    mask_f = nc.dram_tensor("mask_f", [128, 128], F32, kind="ExternalInput")
    outT = nc.dram_tensor("outT", [H, S], F32, kind="ExternalOutput")
    spill = nc.dram_tensor("spill", [H, S], BF16, kind="Internal")

    sch = Sched()
    add = sch.add

    sem_names = ["d_mask", "d_scl", "d_xbf0", "d_xbf1", "d_xst0", "d_xst1", "d_xst2", "d_xst3",
                 "d_wq", "d_wk", "d_wv", "d_wvf", "d_wo0", "d_wo1",
                 "d_wg0", "d_wg1", "d_wu0", "d_wu1", "d_wd0", "d_wd1",
                 "d_xf0", "d_xf1", "d_sf0", "d_sf1", "d_sp0", "d_sp1",
                 "d_ou0", "d_ou1",
                 "pe_stat", "pe_v", "pe_qk", "pe_sc", "pe_pvs", "pe_norm",
                 "pe_wo", "pe_gu", "pe_sy", "pe_dn", "pe_sx",
                 "sc_sq", "sc_ln", "sc_exp", "sc_nsq", "sc_gu", "sc_dn",
                 "v_init", "v_stat", "v_row", "v_bc", "v_app", "v_vevac",
                 "v_qkevac", "v_nred", "v_negm", "v_mask", "v_att",
                 "v_sT", "v_y", "v_ny", "v_fin", "v_nx", "v_rec", "sc_rec", "p_app",
                 "v_pool"]
    g = {nm: Sem(nc, nm) for nm in sem_names}

    per = ExitStack()
    xnT = per.enter_context(nc.sbuf_tensor("xnT", [128, NHT * S], BF16))
    ones_bf = per.enter_context(nc.sbuf_tensor("ones_bf", [128, 128], BF16))
    onesrow_f = per.enter_context(nc.sbuf_tensor("onesrow_f", [1, 128], F32))
    mask_bf = per.enter_context(nc.sbuf_tensor("mask_bf", [128, 128], BF16))
    sg_sb = per.enter_context(nc.sbuf_tensor("sg_sb", [128, NFT], F32))
    su_sb = per.enter_context(nc.sbuf_tensor("su_sb", [128, NFT], F32))
    sd_sb = per.enter_context(nc.sbuf_tensor("sd_sb", [128, 16], F32))
    negM = per.enter_context(nc.sbuf_tensor("negM", [128, 16], F32))
    nm_sc = per.enter_context(nc.sbuf_tensor("nm_sc", [1, 16], F32))
    n128x = per.enter_context(nc.sbuf_tensor("n128x", [1, S], BF16))
    n128y = per.enter_context(nc.sbuf_tensor("n128y", [1, 512], F32))
    wqk_sb = per.enter_context(nc.sbuf_tensor("wqk_sb", [128, 2 * 2048], BF16))
    wv_sb = per.enter_context(nc.sbuf_tensor("wv_sb", [128, 16 * 512], BF16))

    v_ones = add("vector", lambda e: e.memset(ones_bf[:, :], 1.0), inc=(g["v_init"], 1))
    v_onesr = add("vector", lambda e: e.memset(onesrow_f[:, :], 1.0), inc=(g["v_init"], 1))
    d_mask_all = add("gpsimd", lambda e: e.dma_start(mask_bf[:, :], mask_f[:, :]),
                     inc=(g["d_mask"], 16))
    add("sync", lambda e: e.dma_start(sg_sb[:, :], sg[:, :]), inc=(g["d_scl"], 16))
    add("sync", lambda e: e.dma_start(su_sb[:, :], su[:, :]), inc=(g["d_scl"], 16))
    d_scl_all = add("sync", lambda e: e.dma_start(sd_sb[:, :], sd[:, :]),
                    inc=(g["d_scl"], 16))

    wv0_state = {"cast": 0, "piece": 0, "jp": 0}

    def emit_wv0_piece():
        jp = wv0_state["jp"]
        if jp >= 4:
            return
        wv0_state["piece"] = add(
            "sync",
            lambda e, jp=jp: e.dma_start(wvf[:, :],
                                         wv4[0, :, jp * 2048:(jp + 1) * 2048]),
            waits=[(g["d_wv"], wv0_state["cast"])],
            inc=(g["d_wvf"], 16))

    def emit_wv0_cast():
        jp = wv0_state["jp"]
        if jp >= 4:
            return
        wv0_state["cast"] = add(
            "gpsimd",
            lambda e, jp=jp: e.dma_start(wv_sb[:, jp * 2048:(jp + 1) * 2048],
                                         wvf[:, :]),
            waits=[(g["d_wvf"], wv0_state["piece"])],
            inc=(g["d_wv"], 16))
        wv0_state["jp"] = jp + 1

    emit_wv0_piece()

    stps = ExitStack()
    ps_st1 = stps.enter_context(nc.psum_tensor("ps_st1", [1, 4 * 512], F32))
    ps_st2 = stps.enter_context(nc.psum_tensor("ps_st2", [1, 4 * 512], F32))

    lns = ExitStack()
    xbf_t = lns.enter_context(nc.sbuf_tensor("xbf_t", [128, 4 * S], BF16))
    xbf = [xbf_t[:, 0:S], xbf_t[:, S:2 * S], xbf_t[:, 2 * S:3 * S], xbf_t[:, 3 * S:4 * S]]
    xsq = [lns.enter_context(nc.sbuf_tensor(f"xsq{i}", [128, S], BF16)) for i in range(2)]
    xst = [lns.enter_context(nc.sbuf_tensor(f"xst{i}", [128, 512], F32)) for i in range(4)]
    r_a = lns.enter_context(nc.sbuf_tensor("r_a", [1, S], F32))
    r_b = lns.enter_context(nc.sbuf_tensor("r_b", [1, S], F32))
    r_c = lns.enter_context(nc.sbuf_tensor("r_c", [1, S], F32))
    rB = lns.enter_context(nc.sbuf_tensor("rB", [128, S], F32))
    mrB = lns.enter_context(nc.sbuf_tensor("mrB", [128, S], F32))
    t1 = [lns.enter_context(nc.sbuf_tensor(f"t1_{i}", [128, 512], F32)) for i in range(2)]
    wvf = lns.enter_context(nc.sbuf_tensor("wvf", [128, 2048], F32))

    cn_xbf = [0] * NHT
    cn_xsq = [0] * NHT
    cn_stat = [0] * NHT
    for ht in range(NHT):
        p = ht % 4
        if ht % 2 == 0:
            half = (ht // 2) % 2
            cn_xbf[ht] = add(
                "gpsimd",
                lambda e, ht=ht, half=half: e.dma_start(
                    xbf_t[:, half * 2 * S:(half + 1) * 2 * S],
                    xT.rearrange("(n p) s -> p n s", p=128)[:, ht:ht + 2, :]),
                waits=[(g["pe_stat"], cn_stat[ht - 3] if ht >= 3 else 0),
                       (g["sc_sq"], cn_xsq[ht - 3] if ht >= 3 else 0)],
                inc=(g[f"d_xbf{half}"], 16))
            cn_xbf[ht + 1] = cn_xbf[ht]
            if ht >= 2:
                emit_wv0_cast()
                emit_wv0_piece()
        p2 = ht % 2
        cn_xsq[ht] = add(
            "vector",
            lambda e, p=p, p2=p2: e.tensor_mul(xsq[p2][:, :], xbf[p][:, :], xbf[p][:, :]),
            waits=[(g[f"d_xbf{(ht // 2) % 2}"], cn_xbf[ht]),
                   (g["pe_stat"], cn_stat[ht - 2] if ht >= 2 else 0)],
            inc=(g["sc_sq"], 1))
        for c in range(4):
            add("tensor",
                lambda e, p=p, c=c, ht=ht: e.matmul(
                    ps_st1[0:1, c * 512:(c + 1) * 512], ones_bf[:, 0:1],
                    xbf[p][:, c * 512:(c + 1) * 512],
                    start=(ht == 0), stop=(ht == NHT - 1)),
                waits=[(g[f"d_xbf{(ht // 2) % 2}"], cn_xbf[ht]), (g["v_init"], v_ones)])
        for c in range(4):
            r = add(
                "tensor",
                lambda e, p2=ht % 2, c=c, ht=ht: e.matmul(
                    ps_st2[0:1, c * 512:(c + 1) * 512], ones_bf[:, 0:1],
                    xsq[p2][:, c * 512:(c + 1) * 512],
                    start=(ht == 0), stop=(ht == NHT - 1)),
                waits=[(g["sc_sq"], cn_xsq[ht])],
                inc=(g["pe_stat"], 1) if c == 3 else None)
            if r is not None:
                cn_stat[ht] = r

    stat_done = (g["pe_stat"], cn_stat[NHT - 1])
    c1 = 1.0 / 2048.0
    cn_ch_rstd = [0] * 4
    cn_ch_mr = [0] * 4
    for c in range(4):
        sl = slice(c * 512, (c + 1) * 512)
        va = add("vector", lambda e, sl=sl: e.tensor_copy(r_a[0:1, sl], ps_st1[0:1, sl]),
                 waits=[stat_done], inc=(g["v_row"], 1))
        vb = add("vector", lambda e, sl=sl: e.tensor_copy(r_b[0:1, sl], ps_st2[0:1, sl]),
                 waits=[stat_done, (g["v_row"], va)], inc=(g["v_row"], 1))
        vw = add("vector", lambda e, sl=sl: e.tensor_scalar_mul(r_a[0:1, sl],
                                                                r_a[0:1, sl], c1),
                 waits=[(g["v_row"], vb)], inc=(g["v_row"], 1))
        vw = add("vector", lambda e, sl=sl: e.tensor_scalar_mul(r_b[0:1, sl],
                                                                r_b[0:1, sl], c1),
                 waits=[(g["v_row"], vw)], inc=(g["v_row"], 1))
        vc = add("vector", lambda e, sl=sl: e.tensor_mul(r_c[0:1, sl], r_a[0:1, sl],
                                                         r_a[0:1, sl]),
                 waits=[(g["v_row"], vw)], inc=(g["v_row"], 1))
        vd = add("vector", lambda e, sl=sl: e.tensor_sub(r_b[0:1, sl], r_b[0:1, sl],
                                                         r_c[0:1, sl]),
                 waits=[(g["v_row"], vc)], inc=(g["v_row"], 1))
        vd = add("vector", lambda e, sl=sl: e.tensor_scalar_add(r_b[0:1, sl],
                                                                r_b[0:1, sl], EPS),
                 waits=[(g["v_row"], vd)], inc=(g["v_row"], 1))
        sa = add("scalar", lambda e, sl=sl: e.activation(r_b[0:1, sl], r_b[0:1, sl],
                                                         AF.Ln),
                 waits=[(g["v_row"], vd)], inc=(g["sc_ln"], 1))
        cn_ch_rstd[c] = add(
            "scalar", lambda e, sl=sl: e.activation(r_b[0:1, sl], r_b[0:1, sl],
                                                    AF.Exp, scale=-0.5),
            waits=[(g["sc_ln"], sa)], inc=(g["sc_ln"], 1))
        cn_ch_mr[c] = add(
            "vector",
            lambda e, sl=sl: e.scalar_tensor_tensor(r_c[0:1, sl], r_a[0:1, sl], -1.0,
                                                    r_b[0:1, sl], ALU.mult, ALU.mult),
            waits=[(g["sc_ln"], cn_ch_rstd[c])], inc=(g["v_row"], 1))
    sc_rstd = cn_ch_rstd[3]
    v_mr = cn_ch_mr[3]

    stps.close()
    psA = per.enter_context(nc.psum_tensor("psA", [128, 2 * 512], F32))
    ps = [None, None] + [per.enter_context(nc.psum_tensor(f"ps{i}", [128, 512], F32))
                         for i in range(2, 8)]

    bc_specs = []
    for c in range(4):
        bc_specs.append((r_b, rB, g["sc_ln"], cn_ch_rstd[c], c))
        bc_specs.append((r_c, mrB, g["v_row"], cn_ch_mr[c], c))
    bc_mm_cn = [0] * 8
    bc_cp_cn = [0] * 8
    cn_bc_ch = [0] * 4
    for j, (src, dst, wsem, wval, c) in enumerate(bc_specs):
        par = j % 2
        bc_mm_cn[j] = add(
            "tensor",
            lambda e, src=src, c=c, par=par: e.matmul(
                psA[:, par * 512:(par + 1) * 512], onesrow_f[:, :],
                src[0:1, c * 512:(c + 1) * 512],
                start=True, stop=True),
            waits=[(wsem, wval), (g["v_init"], v_onesr),
                   (g["v_bc"], bc_cp_cn[j - 2] if j >= 2 else 0)],
            inc=(g["pe_stat"], 1))
        bc_cp_cn[j] = add(
            "vector",
            lambda e, dst=dst, c=c, par=par: e.tensor_copy(
                dst[:, c * 512:(c + 1) * 512], psA[:, par * 512:(par + 1) * 512]),
            waits=[(g["pe_stat"], bc_mm_cn[j])], inc=(g["v_bc"], 1))
        if par == 1:
            cn_bc_ch[c] = bc_cp_cn[j]
    v_bc_all = bc_cp_cn[7]

    cn_appj = {}
    cn_app_ch = {}
    for c in range(4):
        for ht in range(NHT):
            j = c * NHT + ht
            sl = j % 4
            tp = j % 2
            dcn = add(
                "sync",
                lambda e, sl=sl, c=c, ht=ht: e.dma_start(
                    xst[sl][:, :],
                    xT[ht * 128:(ht + 1) * 128, c * 512:(c + 1) * 512]),
                waits=[(g["v_app"], cn_appj.get(j - 4, 0))],
                inc=(g[f"d_xst{sl}"], 16))
            mcn = add(
                "vector",
                lambda e, sl=sl, tp=tp, c=c: e.tensor_mul(
                    t1[tp][:, :], xst[sl][:, :], rB[:, c * 512:(c + 1) * 512]),
                waits=[(g[f"d_xst{sl}"], dcn), (g["v_bc"], cn_bc_ch[c]),
                       (g["v_app"], cn_appj.get(j - 2, 0))],
                inc=(g["v_app"], 1))
            cn_appj[j] = add(
                "vector",
                lambda e, tp=tp, c=c, ht=ht: e.tensor_add(
                    xnT[:, ht * S + c * 512: ht * S + (c + 1) * 512],
                    t1[tp][:, :], mrB[:, c * 512:(c + 1) * 512]),
                waits=[(g["v_app"], mcn)],
                inc=(g["v_app"], 1))
            cn_app_ch[(c, ht)] = cn_appj[j]
    cn_apply_done = cn_app_ch[(3, NHT - 1)]
    apply_done_waits = [(g["v_app"], g["v_app"].count), (g["p_app"], g["p_app"].count)]

    lns.close()
    att1 = ExitStack()
    attnT = att1.enter_context(nc.sbuf_tensor("attnT", [128, NHEAD * S], BF16))
    att2 = ExitStack()
    qT = att2.enter_context(nc.sbuf_tensor("qT", [128, 2 * S], BF16))
    kT = att2.enter_context(nc.sbuf_tensor("kT", [128, 2 * S], BF16))
    pr = att2.enter_context(nc.sbuf_tensor("pr", [128, PR_SLOTS * 512], BF16))
    v_sb = att2.enter_context(nc.sbuf_tensor("v_sb", [128, 16 * 512], BF16))
    recipB = att2.enter_context(nc.sbuf_tensor("recipB", [128, 2 * 512], F32))
    sqs = att2.enter_context(nc.sbuf_tensor("sqs", [128, S // 2], BF16))
    qpool = att2.enter_context(nc.sbuf_tensor("qpool", [128, 512], BF16))
    kpool = att2.enter_context(nc.sbuf_tensor("kpool", [128, 512], BF16))

    cn_wv = [0] * 4
    cn_vmm = [0] * 4
    cn_vevac = [0] * 4
    cn_pv_group = [0] * 4
    cn_wq = [0] * NHEAD
    cn_wk = [0] * NHEAD
    cn_qkg = {}
    cn_qev = [0] * NHEAD
    cn_kev = [0] * NHEAD
    cn_negm = [0] * NHEAD
    cn_scores_done = [0] * NHEAD
    cn_sq_done = [0] * NHEAD
    exp_cn = {}
    mask_cn = {}
    pv_cn = {}
    cn_att_ev = {}
    qk_par_last_evac = [0, 0]
    cn_norm_mm = [0]
    qpool_cn = [0]
    cn_pool = [0]
    cn_nred_last = [0]
    gkt = [0]
    unit_kts = {}
    pv_pending = deque()

    def emit_vproj(grp):
        base_waits = [(g["d_wv"], cn_wv[grp])]
        if grp > 0:
            u_last = grp * 16 - 1
            base_waits += [(g["v_vevac"], cn_vevac[grp - 1]),
                           (g["v_qkevac"], qk_par_last_evac[0]),
                           (g["v_qkevac"], qk_par_last_evac[1]),
                           (g["v_nred"], cn_nred_last[0]),
                           (g["v_negm"], cn_negm[grp * 4]),
                           cn_rl.get(u_last, (g["v_rec"], 0)),
                           (g["v_att"], cn_att_ev.get(u_last, 0))]
        last_mm = [0]
        rounds = [list(range(0, 6)), list(range(6, 12)), list(range(12, 16))]
        evac_bank = {}
        bank_done = {}
        for r, sts in enumerate(rounds):
            for ht in range(NHT):
                for bi, st in enumerate(sts):
                    bank = 2 + bi
                    w = []
                    if ht == 0:
                        if r == 0 and st == sts[0]:
                            w = list(base_waits)
                        else:
                            w = [(g["v_vevac"], evac_bank.get(bank, 0))]
                    if grp == 0:
                        w.append((g["v_app"], cn_app_ch[(st // 4, ht)]))
                    rr = add(
                        "tensor",
                        lambda e, st=st, ht=ht, bank=bank: e.matmul(
                            ps[bank][:, :],
                            xnT[:, ht * S + st * 128: ht * S + (st + 1) * 128],
                            wv_sb[:, ht * 512:(ht + 1) * 512],
                            start=(ht == 0), stop=(ht == NHT - 1)),
                        waits=w,
                        inc=(g["pe_v"], 1) if ht == NHT - 1 else None)
                    if rr is not None:
                        last_mm[0] = rr
                        bank_done[bank] = rr
            for bi, st in enumerate(sts):
                bank = 2 + bi
                w = [(g["pe_v"], bank_done[bank])]
                if grp > 0:
                    w.append((g["pe_pvs"], cn_pv_group[grp - 1]))
                cn_vevac[grp] = add(
                    "scalar",
                    lambda e, st=st, bank=bank: e.activation(
                        v_sb[:, st * 512:(st + 1) * 512], ps[bank][:, :], AF.Copy),
                    waits=w, inc=(g["v_vevac"], 1))
                evac_bank[bank] = cn_vevac[grp]
        cn_vmm[grp] = last_mm[0]

    def emit_qkproj(i):
        w = [(g["pe_qk"], cn_qkg.get((i - 1, 0, 3), 0))]
        cn_wq[i] = add(
            "gpsimd",
            lambda e, i=i: e.dma_start(wqk_sb[:, 0:2048], wqk[i, :, :]),
            waits=w, inc=(g["d_wq"], 16))
        cn_wk[i] = add(
            "gpsimd",
            lambda e, i=i: e.dma_start(wqk_sb[:, 2048:4096], wqk[16 + i, :, :]),
            waits=[(g["pe_qk"], cn_qkg.get((i - 1, 1, 3), 0))],
            inc=(g["d_wk"], 16))
        par_i = i % 2
        for half in range(2):
            dst = qT if half == 0 else kT
            dsem = g["d_wq"] if half == 0 else g["d_wk"]
            dcn = cn_wq[i] if half == 0 else cn_wk[i]
            for c in range(4):
                gidx = half * 4 + c
                par = gidx % 2
                last = 0
                for ht in range(NHT):
                    w = []
                    if ht == 0:
                        w = [(dsem, dcn),
                             (g["v_qkevac"], qk_par_last_evac[par]),
                             (g["v_nred"], cn_nred_last[0]),
                             (g["v_negm"], cn_negm[i - 1] if i >= 1 else 0),
                             (g["v_vevac"], cn_vevac[
                                 0 if i == 0 else (i // 4 if i % 4 != 0 else i // 4 - 1)]),
                             ] + ([(g["v_app"], cn_app_ch[(c, NHT - 1)])]
                                  if i == 0 else [])
                    r = add(
                        "tensor",
                        lambda e, half=half, c=c, ht=ht, par=par: e.matmul(
                            ps[6 + par][:, :],
                            wqk_sb[:, half * 2048 + ht * 128: half * 2048 + (ht + 1) * 128],
                            xnT[:, ht * S + c * 512: ht * S + (c + 1) * 512],
                            start=(ht == 0), stop=(ht == NHT - 1)),
                        waits=w,
                        inc=(g["pe_qk"], 1) if ht == NHT - 1 else None)
                    if r is not None:
                        last = r
                cn_qkg[(i, half, c)] = last
                wv_ = [(g["pe_qk"], last)]
                if i >= 2:
                    wv_.append((g["pe_sc"], cn_scores_done[i - 2]))
                    wv_.append((g["v_pool"], cn_sq_done[i - 2]))
                ev = add(
                    "vector",
                    lambda e, dst=dst, c=c, par=par, par_i=par_i: e.tensor_copy(
                        dst[:, par_i * S + c * 512: par_i * S + (c + 1) * 512],
                        ps[6 + par][:, :]),
                    waits=wv_, inc=(g["v_qkevac"], 1))
                qk_par_last_evac[par] = ev
                if half == 0:
                    cn_qev[i] = ev
                else:
                    cn_kev[i] = ev

    def _half_norm(src_t, par_i, pool, first_waits):
        last = None
        for h in range(2):
            w = list(first_waits) if h == 0 else [(g["v_pool"], last)]
            sq = add(
                "vector",
                lambda e, par_i=par_i, h=h, src_t=src_t: e.tensor_mul(
                    sqs[:, :],
                    src_t[:, par_i * S + h * 1024: par_i * S + (h + 1) * 1024],
                    src_t[:, par_i * S + h * 1024: par_i * S + (h + 1) * 1024]),
                waits=w, inc=(g["v_pool"], 1))
            p0 = add(
                "vector",
                lambda e, h=h, pool=pool: e.tensor_max(
                    pool[:, h * 256:(h + 1) * 256],
                    sqs[:, 0:1024:4], sqs[:, 1:1024:4]),
                waits=[(g["v_pool"], sq)], inc=(g["v_pool"], 1))
            p1 = add(
                "vector",
                lambda e, h=h, pool=pool: e.tensor_max(
                    pool[:, h * 256:(h + 1) * 256],
                    pool[:, h * 256:(h + 1) * 256], sqs[:, 2:1024:4]),
                waits=[(g["v_pool"], p0)], inc=(g["v_pool"], 1))
            last = add(
                "vector",
                lambda e, h=h, pool=pool: e.tensor_max(
                    pool[:, h * 256:(h + 1) * 256],
                    pool[:, h * 256:(h + 1) * 256], sqs[:, 3:1024:4]),
                waits=[(g["v_pool"], p1)], inc=(g["v_pool"], 1))
        return last

    def emit_norms_q(i):
        par_i = i % 2
        cn_pool[0] = _half_norm(
            qT, par_i, qpool,
            [(g["v_qkevac"], cn_qev[i]), (g["pe_norm"], cn_norm_mm[0]),
             (g["v_pool"], cn_pool[0])])
        qpool_cn[0] = cn_pool[0]
        cn_pool[0] = _half_norm(
            kT, par_i, kpool,
            [(g["v_qkevac"], cn_kev[i]), (g["v_pool"], cn_pool[0]),
             (g["pe_norm"], cn_norm_mm[0])])
        cn_sq_done[i] = cn_pool[0]

    def emit_norms_k(i):
        mm = add(
            "tensor",
            lambda e: e.matmul(ps[7][0:1, :], ones_bf[:, 0:1], qpool[:, :],
                               start=True, stop=True),
            waits=[(g["v_pool"], qpool_cn[0]), (g["v_nred"], cn_nred_last[0]),
                   (g["v_qkevac"], qk_par_last_evac[1]),
                   (g["v_negm"], cn_negm[i - 1] if i >= 1 else 0)],
            inc=(g["pe_norm"], 1))
        cn_norm_mm[0] = mm
        cn_nred_last[0] = add(
            "vector",
            lambda e: e.reduce_max(nm_sc[0:1, 0:1], ps[7][0:1, :], axis=AX.X),
            waits=[(g["pe_norm"], mm), (g["v_nred"], cn_nred_last[0])],
            inc=(g["v_nred"], 1))
        mm = add(
            "tensor",
            lambda e: e.matmul(ps[7][0:1, :], ones_bf[:, 0:1], kpool[:, :],
                               start=True, stop=True),
            waits=[(g["v_pool"], cn_pool[0]), (g["v_nred"], cn_nred_last[0])],
            inc=(g["pe_norm"], 1))
        cn_norm_mm[0] = mm
        nred = add(
            "vector",
            lambda e: e.reduce_max(nm_sc[0:1, 1:2], ps[7][0:1, :], axis=AX.X),
            waits=[(g["pe_norm"], mm), (g["v_nred"], cn_nred_last[0])],
            inc=(g["v_nred"], 1))
        cn_nred_last[0] = add(
            "vector",
            lambda e: e.tensor_add(nm_sc[0:1, 10:11], nm_sc[0:1, 0:1],
                                   nm_sc[0:1, 1:2]),
            waits=[(g["v_nred"], nred)], inc=(g["v_nred"], 1))
        mm = add("tensor",
                 lambda e: e.matmul(ps[7][:, 0:1], onesrow_f[:, :],
                                    nm_sc[0:1, 10:11], start=True, stop=True),
                 waits=[(g["v_nred"], cn_nred_last[0])],
                 inc=(g["pe_norm"], 1))
        cn_norm_mm[0] = mm
        cn_negm[i] = add(
            "vector",
            lambda e, i=i: e.tensor_scalar_mul(negM[:, i:i + 1], ps[7][:, 0:1],
                                               -SCALE / 2.0),
            waits=[(g["pe_norm"], mm)], inc=(g["v_negm"], 1))


    cn_rl = {}

    def pump_pv_one():
        st = pv_pending[0]
        i, qc, uidx = st["i"], st["qc"], st["uidx"]
        kts = st["kts"]
        n = st["n"]
        kt, gi, off, width, slot = kts[n]
        L = st["L"]
        sb = 2 + 2 * (uidx % 2)
        ab = 3 + 2 * (uidx % 2)
        wkt = [(g["sc_exp"], exp_cn[gi])]
        if gi in mask_cn:
            wkt.append((g["v_mask"], mask_cn[gi]))
        if n == 0:
            wkt.append(cn_rl.get(uidx - 2, (g["v_rec"], 0)))
            if uidx % 16 < 2:
                wkt.append((g["v_vevac"], cn_vevac[uidx // 16]))
        add("tensor",
            lambda e, off=off, width=width, slot=slot, n=n, L=L, sb=sb: e.matmul(
                ps[sb][:, off:512], ones_bf[:, :],
                pr[:, slot * 512: slot * 512 + width],
                start=(n == 0), stop=(n == L - 1)),
            waits=wkt)
        hd_loc = i % 4
        wpv = [(g["v_att"], cn_att_ev.get(uidx - 2, 0))] if n == 0 else []
        if n == 0 and uidx % 16 < 2:
            wpv.append((g["v_vevac"], cn_vevac[uidx // 16]))
        pv_cn[gi] = add(
            "tensor",
            lambda e, kt=kt, hd_loc=hd_loc, off=off, width=width, slot=slot,
            n=n, L=L, ab=ab: e.matmul(
                ps[ab][:, off:512],
                v_sb[:, kt * 512 + hd_loc * 128: kt * 512 + (hd_loc + 1) * 128],
                pr[:, slot * 512: slot * 512 + width],
                start=(n == 0), stop=(n == L - 1)),
            waits=wpv,
            inc=(g["pe_pvs"], 1))
        st["n"] += 1
        if st["n"] == L:
            upar = uidx % 2
            if uidx < 64:
                rcl = add("scalar",
                          lambda e, upar=upar, sb=sb: e.activation(
                              recipB[:, upar * 512:(upar + 1) * 512], ps[sb][:, :],
                              AF.Ln),
                          waits=[(g["pe_pvs"], pv_cn[gi]),
                                 (g["v_att"], cn_att_ev.get(uidx - 2, 0))],
                          inc=(g["sc_rec"], 1))
                rc_ = add("scalar",
                          lambda e, upar=upar: e.activation(
                              recipB[:, upar * 512:(upar + 1) * 512],
                              recipB[:, upar * 512:(upar + 1) * 512],
                              AF.Exp, scale=-1.0),
                          waits=[(g["sc_rec"], rcl)],
                          inc=(g["sc_rec"], 1))
                cn_rl[uidx] = (g["sc_rec"], rc_)
            else:
                rc_ = add("vector",
                          lambda e, upar=upar, sb=sb: e.reciprocal(
                              recipB[:, upar * 512:(upar + 1) * 512], ps[sb][:, :]),
                          waits=[(g["pe_pvs"], pv_cn[gi]),
                                 (g["v_att"], cn_att_ev.get(uidx - 2, 0))],
                          inc=(g["v_rec"], 1))
                cn_rl[uidx] = (g["v_rec"], rc_)
            cn_att_ev[uidx] = add(
                "vector",
                lambda e, i=i, qc=qc, upar=upar, ab=ab: e.tensor_tensor(
                    attnT[:, i * S + qc * 512: i * S + (qc + 1) * 512],
                    ps[ab][:, :], recipB[:, upar * 512:(upar + 1) * 512], ALU.mult),
                waits=[cn_rl[uidx]],
                inc=(g["v_att"], 1))
            pv_pending.popleft()

    def pump_all():
        while pv_pending:
            pump_pv_one()

    def emit_scores(i, qc):
        par_i = i % 2
        nkt = 4 * qc + 4
        st_rec = {"i": i, "qc": qc, "uidx": i * 4 + qc, "kts": [], "n": 0, "L": nkt}
        pv_pending.append(st_rec)
        kts = st_rec["kts"]
        for kt in range(nkt):
            gi = gkt[0]
            gkt[0] += 1
            off = max(0, kt * 128 - qc * 512)
            width = 512 - off
            slot = gi % PR_SLOTS
            par = gi % 2
            while gi >= PR_SLOTS and (gi - PR_SLOTS) not in pv_cn:
                pump_pv_one()
            w = [(g["v_qkevac"], max(cn_qev[i], cn_kev[i]))] if kt == 0 else []
            if gi >= 2:
                w.append((g["sc_exp"], exp_cn[gi - 2]))
            cn = add(
                "tensor",
                lambda e, kt=kt, off=off, par=par, par_i=par_i, qc=qc: e.matmul(
                    psA[:, par * 512 + off: par * 512 + 512],
                    kT[:, par_i * S + kt * 128: par_i * S + (kt + 1) * 128],
                    qT[:, par_i * S + qc * 512 + off: par_i * S + (qc + 1) * 512],
                    start=True, stop=True),
                waits=w, inc=(g["pe_sc"], 1))
            cn_scores_done[i] = cn
            we = [(g["pe_sc"], cn)]
            if kt == 0:
                we.append((g["v_negm"], cn_negm[i]))
            if gi >= PR_SLOTS:
                we.append((g["pe_pvs"], pv_cn[gi - PR_SLOTS]))
            exp_cn[gi] = add(
                "scalar",
                lambda e, slot=slot, width=width, off=off, par=par, i=i: e.activation(
                    pr[:, slot * 512: slot * 512 + width],
                    psA[:, par * 512 + off: par * 512 + 512],
                    AF.Exp, bias=negM[:, i:i + 1], scale=SCALE),
                waits=we, inc=(g["sc_exp"], 1))
            if kt >= 4 * qc:
                mask_cn[gi] = add(
                    "vector",
                    lambda e, slot=slot: e.tensor_mul(
                        pr[:, slot * 512: slot * 512 + 128],
                        pr[:, slot * 512: slot * 512 + 128], mask_bf[:, :]),
                    waits=[(g["sc_exp"], exp_cn[gi]), (g["d_mask"], d_mask_all)],
                    inc=(g["v_mask"], 1))
            kts.append((kt, gi, off, width, slot))
        unit_kts[(i, qc)] = kts

    cn_wv[0] = wv0_state["cast"]
    assert wv0_state["jp"] == 4 and cn_wv[0] == 64
    emit_vproj(0)
    emit_qkproj(0)
    emit_norms_q(0)
    emit_norms_k(0)
    for i in range(NHEAD):
        grp = i // 4
        for qc in range(4):
            if qc == 3 and i + 1 < NHEAD:
                emit_norms_q(i + 1)
            emit_scores(i, qc)
            if qc == 2 and i + 1 < NHEAD:
                emit_qkproj(i + 1)
        if i + 1 < NHEAD:
            emit_norms_k(i + 1)
        if i % 4 == 3:
            pump_all()
            cn_pv_group[grp] = g["pe_pvs"].count
            if grp + 1 < 4:
                cn_wv[grp + 1] = add(
                    "gpsimd",
                    lambda e, grp=grp: e.dma_start(wv_sb[:, :], wv4[grp + 1, :, :]),
                    waits=[(g["pe_v"], cn_vmm[grp])],
                    inc=(g["d_wv"], 16))
                emit_vproj(grp + 1)
    pump_all()
    cn_att_final = g["v_att"].count
    cn_rec_final = g["v_rec"].count
    cn_screc_final = g["sc_rec"].count
    cn_exp_final = g["sc_exp"].count
    cn_sc_final = g["pe_sc"].count
    cn_pvs_final = g["pe_pvs"].count
    cn_nsq_final = g["v_pool"].count
    cn_vevac_final = g["v_vevac"].count

    att2.close()
    wos = ExitStack()
    sT_st = [wos.enter_context(nc.sbuf_tensor(f"sT{i}", [128, S], BF16)) for i in range(2)]

    cn_wod = [0] * 16
    cn_wog = {}
    cn_sTev = [0] * 16
    cn_spill = [0] * 16
    wo_par_last_evac = [0, 0]
    first_sT = True
    for ot in range(16):
        w = [(g["pe_qk"], cn_qkg[(NHEAD - 1, 1, 3)])] if ot < 2 else \
            [(g["pe_wo"], cn_wog[(ot - 2, 3)])]
        cn_wod[ot] = add(
            "gpsimd",
            lambda e, ot=ot: e.dma_start(wqk_sb[:, (ot % 2) * 2048:(ot % 2 + 1) * 2048],
                                         wo[ot, :, :]),
            waits=w, inc=(g[f"d_wo{ot % 2}"], 16))
        for c in range(4):
            par = (ot * 4 + c) % 2
            last = 0
            for ht in range(NHT):
                w = []
                if ht == 0:
                    w = [(g[f"d_wo{ot % 2}"], cn_wod[ot]), (g["v_att"], cn_att_final),
                         (g["v_sT"], wo_par_last_evac[par]),
                         (g["v_negm"], cn_negm[NHEAD - 1]),
                         (g["v_nred"], cn_nred_last[0]),
                         (g["v_rec"], cn_rec_final),
                         (g["sc_rec"], cn_screc_final)]
                r = add(
                    "tensor",
                    lambda e, ot=ot, c=c, ht=ht, par=par: e.matmul(
                        ps[4 + par][:, :],
                        wqk_sb[:, (ot % 2) * 2048 + ht * 128: (ot % 2) * 2048 + (ht + 1) * 128],
                        attnT[:, ht * S + c * 512: ht * S + (c + 1) * 512],
                        start=(ht == 0), stop=(ht == NHT - 1)),
                    waits=w,
                    inc=(g["pe_wo"], 1) if ht == NHT - 1 else None)
                if r is not None:
                    last = r
            cn_wog[(ot, c)] = last
            w = [(g["pe_wo"], last)]
            if first_sT:
                w += [(g["pe_sc"], cn_sc_final), (g["v_pool"], cn_nsq_final),
                      (g["pe_pvs"], cn_pvs_final), (g["sc_exp"], cn_exp_final)]
                first_sT = False
            if ot >= 2:
                w.append((g[f"d_sp{ot % 2}"], cn_spill[ot - 2]))
            ev = add(
                "vector",
                lambda e, ot=ot, c=c, par=par: e.tensor_copy(
                    sT_st[ot % 2][:, c * 512:(c + 1) * 512], ps[4 + par][:, :]),
                waits=w, inc=(g["v_sT"], 1))
            wo_par_last_evac[par] = ev
            cn_sTev[ot] = ev
        cn_spill[ot] = add(
            "sync",
            lambda e, ot=ot: e.dma_start(spill[ot * 128:(ot + 1) * 128, :],
                                         sT_st[ot % 2][:, :]),
            waits=[(g["v_sT"], cn_sTev[ot])], inc=(g[f"d_sp{ot % 2}"], 16))
    cn_wo_final = g["pe_wo"].count
    cn_vsT_final = g["v_sT"].count
    cn_spill_final = cn_spill[15]

    wos.close()
    att1.close()
    ffs = ExitStack()
    wg_sb = ffs.enter_context(nc.sbuf_tensor("wg_sb", [128, 2 * 2048], BF16))
    wu_sb = ffs.enter_context(nc.sbuf_tensor("wu_sb", [128, 2 * 2048], BF16))
    wd_sb = ffs.enter_context(nc.sbuf_tensor("wd_sb", [128, 2 * NFT * 128], BF16))
    y_sb = ffs.enter_context(nc.sbuf_tensor("y_sb", [128, NFT * 512], BF16))
    tmpg = ffs.enter_context(nc.sbuf_tensor("tmpg", [128, 2 * 512], BF16))
    tmpu = ffs.enter_context(nc.sbuf_tensor("tmpu", [128, 2 * 512], BF16))
    gf = ffs.enter_context(nc.sbuf_tensor("gf", [128, 2 * 512], F32))
    xf = ffs.enter_context(nc.sbuf_tensor("xf", [128, 2 * 512], F32))
    sf = ffs.enter_context(nc.sbuf_tensor("sf", [128, 2 * 512], F32))
    of = ffs.enter_context(nc.sbuf_tensor("of", [128, 2 * 512], F32))

    coarse_ffn = [(g["pe_wo"], cn_wo_final), (g["v_sT"], cn_vsT_final),
                  (g["d_sp0"], cn_spill[14]), (g["d_sp1"], cn_spill[15]),
                  (g["pe_pvs"], cn_pvs_final), (g["v_rec"], cn_rec_final),
                  (g["sc_rec"], cn_screc_final),
                  (g["sc_exp"], cn_exp_final), (g["v_vevac"], cn_vevac_final)]

    def prev2(d, sc, k, percol):
        if k >= 2:
            return d.get((sc, k - 2), 0)
        if sc > 0:
            return d.get((sc - 1, percol - 2 + k), 0)
        return 0

    cn_wgd = {}
    cn_wud = {}
    cn_wdd = {}
    cn_gug = {}
    cn_uug = {}
    cn_sg_ev = {}
    cn_su_ev = {}
    cn_y = {}
    cn_ny = [0] * 4
    cn_dng = {}
    cn_gsil = {}
    cn_fin = {}
    cn_xfd = {}
    cn_sfd = {}
    cn_outd = {}

    for sc in range(4):
        for ht in range(NFT):
            hp = ht % 2
            w = [(g["pe_gu"], prev2(cn_gug, sc, ht, NFT))]
            if sc == 0 and ht < 2:
                w = list(coarse_ffn)
            cn_wgd[(sc, ht)] = add(
                "gpsimd",
                lambda e, ht=ht, hp=hp: e.dma_start(
                    wg_sb[:, hp * 2048:(hp + 1) * 2048], wg[ht, :, :]),
                waits=w, inc=(g[f"d_wg{hp}"], 16))
            w = [(g["pe_gu"], prev2(cn_uug, sc, ht, NFT))]
            if sc == 0 and ht < 2:
                w = list(coarse_ffn)
            cn_wud[(sc, ht)] = add(
                "gpsimd",
                lambda e, ht=ht, hp=hp: e.dma_start(
                    wu_sb[:, hp * 2048:(hp + 1) * 2048], wu[ht, :, :]),
                waits=w, inc=(g[f"d_wu{hp}"], 16))
            for dt in range(NHT):
                w = []
                if dt == 0:
                    w = [(g[f"d_wg{hp}"], cn_wgd[(sc, ht)]),
                         (g["sc_gu"], prev2(cn_sg_ev, sc, ht, NFT))]
                r = add("tensor",
                        lambda e, ht=ht, dt=dt, hp=hp, sc=sc: e.matmul(
                            psA[:, hp * 512:(hp + 1) * 512],
                            wg_sb[:, hp * 2048 + dt * 128: hp * 2048 + (dt + 1) * 128],
                            xnT[:, dt * S + sc * 512: dt * S + (sc + 1) * 512],
                            start=(dt == 0), stop=(dt == NHT - 1)),
                        waits=w,
                        inc=(g["pe_gu"], 1) if dt == NHT - 1 else None)
            cn_gug[(sc, ht)] = r
            for dt in range(NHT):
                w = []
                if dt == 0:
                    w = [(g[f"d_wu{hp}"], cn_wud[(sc, ht)]),
                         (g["sc_gu"], prev2(cn_su_ev, sc, ht, NFT))]
                r = add("tensor",
                        lambda e, ht=ht, dt=dt, hp=hp, sc=sc: e.matmul(
                            ps[2 + hp][:, :],
                            wu_sb[:, hp * 2048 + dt * 128: hp * 2048 + (dt + 1) * 128],
                            xnT[:, dt * S + sc * 512: dt * S + (sc + 1) * 512],
                            start=(dt == 0), stop=(dt == NHT - 1)),
                        waits=w,
                        inc=(g["pe_gu"], 1) if dt == NHT - 1 else None)
            cn_uug[(sc, ht)] = r
            cn_sg_ev[(sc, ht)] = add(
                "scalar",
                lambda e, hp=hp, ht=ht: e.activation(
                    tmpg[:, hp * 512:(hp + 1) * 512],
                    psA[:, hp * 512:(hp + 1) * 512], AF.Silu,
                    scale=sg_sb[:, ht:ht + 1]),
                waits=[(g["pe_gu"], cn_gug[(sc, ht)]), (g["d_scl"], d_scl_all),
                       (g["v_y"], prev2(cn_y, sc, ht, NFT))],
                inc=(g["sc_gu"], 1))
            cn_su_ev[(sc, ht)] = add(
                "scalar",
                lambda e, hp=hp, ht=ht: e.activation(
                    tmpu[:, hp * 512:(hp + 1) * 512], ps[2 + hp][:, :], AF.Copy,
                    scale=su_sb[:, ht:ht + 1]),
                waits=[(g["pe_gu"], cn_uug[(sc, ht)]),
                       (g["v_y"], prev2(cn_y, sc, ht, NFT))],
                inc=(g["sc_gu"], 1))
            w = [(g["sc_gu"], cn_su_ev[(sc, ht)])]
            if ht == 0 and sc > 0:
                w.append((g["pe_dn"], cn_dng[(sc - 1, 15)]))
            cn_y[(sc, ht)] = add(
                "vector",
                lambda e, hp=hp, ht=ht: e.tensor_mul(
                    y_sb[:, ht * 512:(ht + 1) * 512],
                    tmpg[:, hp * 512:(hp + 1) * 512],
                    tmpu[:, hp * 512:(hp + 1) * 512]),
                waits=w, inc=(g["v_y"], 1))
        for dt in range(16):
            dp = dt % 2
            w = [(g["pe_dn"], prev2(cn_dng, sc, dt, 16))]
            if sc == 0 and dt < 2:
                w = list(coarse_ffn)
            cn_wdd[(sc, dt)] = add(
                "gpsimd",
                lambda e, dt=dt, dp=dp: e.dma_start(
                    wd_sb[:, dp * NFT * 128:(dp + 1) * NFT * 128], wdn[dt, :, :]),
                waits=w, inc=(g[f"d_wd{dp}"], 16))
            cn_xfd[(sc, dt)] = add(
                "sync",
                lambda e, sc=sc, dt=dt, dp=dp: e.dma_start(
                    xf[:, dp * 512:(dp + 1) * 512],
                    xT[dt * 128:(dt + 1) * 128, sc * 512:(sc + 1) * 512]),
                waits=[(g["v_fin"], prev2(cn_fin, sc, dt, 16))],
                inc=(g[f"d_xf{dp}"], 16))
            cn_sfd[(sc, dt)] = add(
                "gpsimd",
                lambda e, sc=sc, dt=dt, dp=dp: e.dma_start(
                    sf[:, dp * 512:(dp + 1) * 512],
                    spill[dt * 128:(dt + 1) * 128, sc * 512:(sc + 1) * 512]),
                waits=[(g[f"d_sp{dt % 2}"], cn_spill[dt]),
                       (g["v_fin"], prev2(cn_fin, sc, dt, 16))],
                inc=(g[f"d_sf{dp}"], 16))
            for ht in range(NFT):
                w = []
                if ht == 0:
                    w = [(g[f"d_wd{dp}"], cn_wdd[(sc, dt)]),
                         (g["v_y"], cn_y[(sc, NFT - 1)]),
                         (g["sc_dn"], prev2(cn_gsil, sc, dt, 16))]
                r = add("tensor",
                        lambda e, dt=dt, ht=ht, dp=dp: e.matmul(
                            ps[4 + dp][:, :],
                            wd_sb[:, dp * NFT * 128 + ht * 128: dp * NFT * 128 + (ht + 1) * 128],
                            y_sb[:, ht * 512:(ht + 1) * 512],
                            start=(ht == 0), stop=(ht == NFT - 1)),
                        waits=w,
                        inc=(g["pe_dn"], 1) if ht == NFT - 1 else None)
            cn_dng[(sc, dt)] = r
            cn_gsil[(sc, dt)] = add(
                "scalar",
                lambda e, dp=dp, dt=dt: e.activation(
                    gf[:, dp * 512:(dp + 1) * 512], ps[4 + dp][:, :], AF.Silu,
                    scale=sd_sb[:, dt:dt + 1]),
                waits=[(g["pe_dn"], cn_dng[(sc, dt)]),
                       (g["v_fin"], prev2(cn_fin, sc, dt, 16))],
                inc=(g["sc_dn"], 1))
            fmul = add("vector",
                       lambda e, dp=dp: e.tensor_mul(of[:, dp * 512:(dp + 1) * 512],
                                                     gf[:, dp * 512:(dp + 1) * 512],
                                                     sf[:, dp * 512:(dp + 1) * 512]),
                       waits=[(g["sc_dn"], cn_gsil[(sc, dt)]),
                              (g[f"d_sf{dp}"], cn_sfd[(sc, dt)]),
                              (g[f"d_ou{dp}"], prev2(cn_outd, sc, dt, 16))],
                       inc=(g["v_fin"], 1))
            cn_fin[(sc, dt)] = add(
                "vector",
                lambda e, dp=dp: e.tensor_add(of[:, dp * 512:(dp + 1) * 512],
                                              of[:, dp * 512:(dp + 1) * 512],
                                              xf[:, dp * 512:(dp + 1) * 512]),
                waits=[(g[f"d_xf{dp}"], cn_xfd[(sc, dt)]), (g["v_fin"], fmul)],
                inc=(g["v_fin"], 1))
            cn_outd[(sc, dt)] = add(
                "sync",
                lambda e, sc=sc, dt=dt, dp=dp: e.dma_start(
                    outT[dt * 128:(dt + 1) * 128, sc * 512:(sc + 1) * 512],
                    of[:, dp * 512:(dp + 1) * 512]),
                waits=[(g["v_fin"], cn_fin[(sc, dt)])],
                inc=(g[f"d_ou{dp}"], 16))

    add("sync", None, waits=[(g["d_ou0"], 32 * 16), (g["d_ou1"], 32 * 16)])

    with nc.Block() as block:
        @block.sync
        def _(e):
            sch.emit("sync", e)

        @block.gpsimd
        def _(e):
            sch.emit("gpsimd", e)

        @block.tensor
        def _(e):
            sch.emit("tensor", e)

        @block.scalar
        def _(e):
            sch.emit("scalar", e)

        @block.vector
        def _(e):
            sch.emit("vector", e)

    ffs.close()
    per.close()
    return nc


_NC_CACHE = {}


def _get_nc():
    if "nc" not in _NC_CACHE:
        _NC_CACHE["nc"] = build_nc()
    return _NC_CACHE["nc"]


def _prep_weights(w_qkv, w_o, wq_gate, s_gate, wq_up, s_up, wq_down, s_down):
    f32 = np.float32
    w_qkv = np.asarray(w_qkv, dtype=f32)
    w_o = np.asarray(w_o, dtype=f32)
    wqk = np.ascontiguousarray(
        w_qkv[:4096].reshape(32, 128, 16, 128).transpose(0, 3, 2, 1)
    ).reshape(32, 128, 2048)
    wv4 = np.ascontiguousarray(
        w_qkv[4096:].reshape(4, 512, 16, 128).transpose(0, 3, 2, 1)
    ).reshape(4, 128, 16 * 512)
    wo_t = np.ascontiguousarray(
        w_o.reshape(16, 128, 16, 128).transpose(0, 3, 2, 1)
    ).reshape(16, 128, 2048)
    def i8zp(a):
        return (np.asarray(a).astype(np.uint8) ^ np.uint8(0x80)).view(np.int8)
    wg_t = np.ascontiguousarray(
        i8zp(wq_gate).reshape(44, 128, 16, 128).transpose(0, 3, 2, 1)
    ).reshape(44, 128, 2048)
    wu_t = np.ascontiguousarray(
        i8zp(wq_up).reshape(44, 128, 16, 128).transpose(0, 3, 2, 1)
    ).reshape(44, 128, 2048)
    wd_t = np.ascontiguousarray(
        i8zp(wq_down).reshape(16, 128, 44, 128).transpose(0, 3, 2, 1)
    ).reshape(16, 128, 44 * 128)
    sg_t = np.ascontiguousarray(np.asarray(s_gate, dtype=f32).reshape(44, 128).T)
    su_t = np.ascontiguousarray(np.asarray(s_up, dtype=f32).reshape(44, 128).T)
    sd_t = np.ascontiguousarray(np.asarray(s_down, dtype=f32).reshape(16, 128).T)
    mask = np.ascontiguousarray(np.triu(np.ones((128, 128), dtype=f32)))
    return dict(wqk=wqk, wv4=wv4, wo=wo_t, wg=wg_t, wu=wu_t, wdn=wd_t,
                sg=sg_t, su=su_t, sd=sd_t, mask_f=mask)


def kernel(L0_attn_hidden_states, L0_attn_w_qkv, L0_attn_w_o,
           L0_ffn_wq_gate, L0_ffn_s_gate, L0_ffn_wq_up, L0_ffn_s_up,
           L0_ffn_wq_down, L0_ffn_s_down):
    x = np.asarray(L0_attn_hidden_states, dtype=np.float32)
    B = x.shape[0]
    shared = _prep_weights(L0_attn_w_qkv, L0_attn_w_o,
                           L0_ffn_wq_gate, L0_ffn_s_gate,
                           L0_ffn_wq_up, L0_ffn_s_up,
                           L0_ffn_wq_down, L0_ffn_s_down)
    in_maps = []
    for b in range(B):
        m = dict(shared)
        m["xT"] = np.ascontiguousarray(x[b].T)
        in_maps.append(m)
    nc = _get_nc()
    res = run_bass_kernel_spmd(nc, in_maps, core_ids=list(range(B)))
    out = np.empty((B, S, H), dtype=np.float32)
    for b in range(B):
        out[b] = np.asarray(res.results[b]["outT"]).T
    return out



# version 49 (final)
# speedup vs baseline: 1.0084x; 1.0084x over previous
import numpy as np
from collections import deque
import concourse.bass as bass
import concourse.mybir as mybir
from concourse.bass_utils import run_bass_kernel_spmd
from contextlib import ExitStack

F32 = mybir.dt.float32
BF16 = mybir.dt.bfloat16
U8 = mybir.dt.uint8
I8 = mybir.dt.int8
AF = mybir.ActivationFunctionType
ALU = mybir.AluOpType
AX = mybir.AxisListType

H = 2048
S = 2048
NHEAD = 16
NHT = 16
NFT = 44
SCALE = float(1.0 / np.sqrt(128.0))
EPS = 1e-5
PR_SLOTS = 7


class Sem:
    def __init__(self, nc, name):
        self.h = nc.alloc_semaphore(name=name)
        self.count = 0
        self.name = name


COMPUTE_ENGS = ("tensor", "scalar", "vector")


class Sched:

    def __init__(self):
        self.items = []
        self.floor = {}
        self.einc = {}

    def add(self, engine, emit_fn, waits=(), inc=None):
        fl = self.floor.setdefault(engine, {})
        ei = self.einc.setdefault(engine, {})
        wd = {}
        for sem, val in waits:
            if val is None or val <= 0:
                continue
            assert val <= sem.count, (
                f"wait {sem.name}>={val} but only {sem.count} scheduled")
            if engine in COMPUTE_ENGS and ei.get(sem, 0) >= val:
                continue
            if fl.get(sem, 0) >= val:
                continue
            wd[sem] = max(wd.get(sem, 0), val)
        for sem, val in wd.items():
            fl[sem] = val
        n_inc = None
        if inc is not None:
            sem, n = inc
            sem.count += n
            n_inc = (sem, n)
            ei[sem] = ei.get(sem, 0) + n
        self.items.append((engine, wd, emit_fn, n_inc))
        return inc[0].count if inc else None

    def emit(self, name, eng):
        for engine, wd, fn, n_inc in self.items:
            if engine != name:
                continue
            for sem, val in wd.items():
                eng.wait_ge(sem.h, val)
            if fn is None:
                continue
            inst = fn(eng)
            if n_inc is not None:
                inst.then_inc(n_inc[0].h, n_inc[1])


def build_nc():
    nc = bass.Bass("TRN2", target_bir_lowering=False)

    xT = nc.dram_tensor("xT", [H, S], F32, kind="ExternalInput")
    wqk = nc.dram_tensor("wqk", [32, 128, NHT * 128], F32, kind="ExternalInput")
    wv4 = nc.dram_tensor("wv4", [4, 128, NHT * 512], F32, kind="ExternalInput")
    wo = nc.dram_tensor("wo", [16, 128, NHT * 128], F32, kind="ExternalInput")
    wg = nc.dram_tensor("wg", [NFT, 128, 16 * 128], I8, kind="ExternalInput")
    wu = nc.dram_tensor("wu", [NFT, 128, 16 * 128], I8, kind="ExternalInput")
    wdn = nc.dram_tensor("wdn", [16, 128, NFT * 128], I8, kind="ExternalInput")
    sg = nc.dram_tensor("sg", [128, NFT], F32, kind="ExternalInput")
    su = nc.dram_tensor("su", [128, NFT], F32, kind="ExternalInput")
    sd = nc.dram_tensor("sd", [128, 16], F32, kind="ExternalInput")
    mask_f = nc.dram_tensor("mask_f", [128, 128], F32, kind="ExternalInput")
    outT = nc.dram_tensor("outT", [H, S], F32, kind="ExternalOutput")
    spill = nc.dram_tensor("spill", [H, S], BF16, kind="Internal")

    sch = Sched()
    add = sch.add

    sem_names = ["d_mask", "d_scl", "d_xbf0", "d_xbf1", "d_xst0", "d_xst1", "d_xst2", "d_xst3",
                 "d_wq", "d_wk", "d_wv", "d_wvf", "d_wo0", "d_wo1",
                 "d_wg0", "d_wg1", "d_wu0", "d_wu1", "d_wd0", "d_wd1",
                 "d_xf0", "d_xf1", "d_sf0", "d_sf1", "d_sp0", "d_sp1",
                 "d_ou0", "d_ou1",
                 "pe_stat", "pe_v", "pe_qk", "pe_sc", "pe_pvs", "pe_norm",
                 "pe_wo", "pe_gu", "pe_sy", "pe_dn", "pe_sx",
                 "sc_sq", "sc_ln", "sc_exp", "sc_nsq", "sc_gu", "sc_dn",
                 "v_init", "v_stat", "v_row", "v_bc", "v_app", "v_vevac",
                 "v_qkevac", "v_nred", "v_negm", "v_mask", "v_att",
                 "v_sT", "v_y", "v_ny", "v_fin", "v_nx", "v_rec", "sc_rec", "p_app",
                 "v_pool"]
    g = {nm: Sem(nc, nm) for nm in sem_names}

    per = ExitStack()
    xnT = per.enter_context(nc.sbuf_tensor("xnT", [128, NHT * S], BF16))
    ones_bf = per.enter_context(nc.sbuf_tensor("ones_bf", [128, 128], BF16))
    onesrow_f = per.enter_context(nc.sbuf_tensor("onesrow_f", [1, 128], F32))
    mask_bf = per.enter_context(nc.sbuf_tensor("mask_bf", [128, 128], BF16))
    sg_sb = per.enter_context(nc.sbuf_tensor("sg_sb", [128, NFT], F32))
    su_sb = per.enter_context(nc.sbuf_tensor("su_sb", [128, NFT], F32))
    sd_sb = per.enter_context(nc.sbuf_tensor("sd_sb", [128, 16], F32))
    negM = per.enter_context(nc.sbuf_tensor("negM", [128, 16], F32))
    nm_sc = per.enter_context(nc.sbuf_tensor("nm_sc", [1, 16], F32))
    n128x = per.enter_context(nc.sbuf_tensor("n128x", [1, S], BF16))
    n128y = per.enter_context(nc.sbuf_tensor("n128y", [1, 512], F32))
    wqk_sb = per.enter_context(nc.sbuf_tensor("wqk_sb", [128, 2 * 2048], BF16))
    wv_sb = per.enter_context(nc.sbuf_tensor("wv_sb", [128, 16 * 512], BF16))

    v_ones = add("vector", lambda e: e.memset(ones_bf[:, :], 1.0), inc=(g["v_init"], 1))
    v_onesr = add("vector", lambda e: e.memset(onesrow_f[:, :], 1.0), inc=(g["v_init"], 1))
    d_mask_all = add("gpsimd", lambda e: e.dma_start(mask_bf[:, :], mask_f[:, :]),
                     inc=(g["d_mask"], 16))
    add("sync", lambda e: e.dma_start(sg_sb[:, :], sg[:, :]), inc=(g["d_scl"], 16))
    add("sync", lambda e: e.dma_start(su_sb[:, :], su[:, :]), inc=(g["d_scl"], 16))
    d_scl_all = add("sync", lambda e: e.dma_start(sd_sb[:, :], sd[:, :]),
                    inc=(g["d_scl"], 16))

    wv0_state = {"cast": 0, "piece": 0, "jp": 0}

    def emit_wv0_piece():
        jp = wv0_state["jp"]
        if jp >= 4:
            return
        wv0_state["piece"] = add(
            "sync",
            lambda e, jp=jp: e.dma_start(wvf[:, :],
                                         wv4[0, :, jp * 2048:(jp + 1) * 2048]),
            waits=[(g["d_wv"], wv0_state["cast"])],
            inc=(g["d_wvf"], 16))

    def emit_wv0_cast():
        jp = wv0_state["jp"]
        if jp >= 4:
            return
        wv0_state["cast"] = add(
            "gpsimd",
            lambda e, jp=jp: e.dma_start(wv_sb[:, jp * 2048:(jp + 1) * 2048],
                                         wvf[:, :]),
            waits=[(g["d_wvf"], wv0_state["piece"])],
            inc=(g["d_wv"], 16))
        wv0_state["jp"] = jp + 1

    emit_wv0_piece()

    stps = ExitStack()
    ps_st1 = stps.enter_context(nc.psum_tensor("ps_st1", [1, 4 * 512], F32))
    ps_st2 = stps.enter_context(nc.psum_tensor("ps_st2", [1, 4 * 512], F32))

    lns = ExitStack()
    xbf_t = lns.enter_context(nc.sbuf_tensor("xbf_t", [128, 4 * S], BF16))
    xbf = [xbf_t[:, 0:S], xbf_t[:, S:2 * S], xbf_t[:, 2 * S:3 * S], xbf_t[:, 3 * S:4 * S]]
    xsq = [lns.enter_context(nc.sbuf_tensor(f"xsq{i}", [128, S], BF16)) for i in range(2)]
    xst = [lns.enter_context(nc.sbuf_tensor(f"xst{i}", [128, 512], F32)) for i in range(4)]
    r_a = lns.enter_context(nc.sbuf_tensor("r_a", [1, S], F32))
    r_b = lns.enter_context(nc.sbuf_tensor("r_b", [1, S], F32))
    r_c = lns.enter_context(nc.sbuf_tensor("r_c", [1, S], F32))
    rB = lns.enter_context(nc.sbuf_tensor("rB", [128, S], F32))
    mrB = lns.enter_context(nc.sbuf_tensor("mrB", [128, S], F32))
    t1 = [lns.enter_context(nc.sbuf_tensor(f"t1_{i}", [128, 512], F32)) for i in range(2)]
    wvf = lns.enter_context(nc.sbuf_tensor("wvf", [128, 2048], F32))

    cn_xbf = [0] * NHT
    cn_xsq = [0] * NHT
    cn_stat = [0] * NHT
    for ht in range(NHT):
        p = ht % 4
        if ht % 2 == 0:
            half = (ht // 2) % 2
            cn_xbf[ht] = add(
                "gpsimd",
                lambda e, ht=ht, half=half: e.dma_start(
                    xbf_t[:, half * 2 * S:(half + 1) * 2 * S],
                    xT.rearrange("(n p) s -> p n s", p=128)[:, ht:ht + 2, :]),
                waits=[(g["pe_stat"], cn_stat[ht - 3] if ht >= 3 else 0),
                       (g["sc_sq"], cn_xsq[ht - 3] if ht >= 3 else 0)],
                inc=(g[f"d_xbf{half}"], 16))
            cn_xbf[ht + 1] = cn_xbf[ht]
            if ht >= 2:
                emit_wv0_cast()
                emit_wv0_piece()
        p2 = ht % 2
        cn_xsq[ht] = add(
            "vector",
            lambda e, p=p, p2=p2: e.tensor_mul(xsq[p2][:, :], xbf[p][:, :], xbf[p][:, :]),
            waits=[(g[f"d_xbf{(ht // 2) % 2}"], cn_xbf[ht]),
                   (g["pe_stat"], cn_stat[ht - 2] if ht >= 2 else 0)],
            inc=(g["sc_sq"], 1))
        for c in range(4):
            add("tensor",
                lambda e, p=p, c=c, ht=ht: e.matmul(
                    ps_st1[0:1, c * 512:(c + 1) * 512], ones_bf[:, 0:1],
                    xbf[p][:, c * 512:(c + 1) * 512],
                    start=(ht == 0), stop=(ht == NHT - 1)),
                waits=[(g[f"d_xbf{(ht // 2) % 2}"], cn_xbf[ht]), (g["v_init"], v_ones)])
        for c in range(4):
            r = add(
                "tensor",
                lambda e, p2=ht % 2, c=c, ht=ht: e.matmul(
                    ps_st2[0:1, c * 512:(c + 1) * 512], ones_bf[:, 0:1],
                    xsq[p2][:, c * 512:(c + 1) * 512],
                    start=(ht == 0), stop=(ht == NHT - 1)),
                waits=[(g["sc_sq"], cn_xsq[ht])],
                inc=(g["pe_stat"], 1) if c == 3 else None)
            if r is not None:
                cn_stat[ht] = r

    stat_done = (g["pe_stat"], cn_stat[NHT - 1])
    c1 = 1.0 / 2048.0
    cn_ch_rstd = [0] * 4
    cn_ch_mr = [0] * 4
    for c in range(4):
        sl = slice(c * 512, (c + 1) * 512)
        va = add("vector", lambda e, sl=sl: e.tensor_copy(r_a[0:1, sl], ps_st1[0:1, sl]),
                 waits=[stat_done], inc=(g["v_row"], 1))
        vb = add("vector", lambda e, sl=sl: e.tensor_copy(r_b[0:1, sl], ps_st2[0:1, sl]),
                 waits=[stat_done, (g["v_row"], va)], inc=(g["v_row"], 1))
        vw = add("vector", lambda e, sl=sl: e.tensor_scalar_mul(r_a[0:1, sl],
                                                                r_a[0:1, sl], c1),
                 waits=[(g["v_row"], vb)], inc=(g["v_row"], 1))
        vw = add("vector", lambda e, sl=sl: e.tensor_scalar_mul(r_b[0:1, sl],
                                                                r_b[0:1, sl], c1),
                 waits=[(g["v_row"], vw)], inc=(g["v_row"], 1))
        vc = add("vector", lambda e, sl=sl: e.tensor_mul(r_c[0:1, sl], r_a[0:1, sl],
                                                         r_a[0:1, sl]),
                 waits=[(g["v_row"], vw)], inc=(g["v_row"], 1))
        vd = add("vector", lambda e, sl=sl: e.tensor_sub(r_b[0:1, sl], r_b[0:1, sl],
                                                         r_c[0:1, sl]),
                 waits=[(g["v_row"], vc)], inc=(g["v_row"], 1))
        vd = add("vector", lambda e, sl=sl: e.tensor_scalar_add(r_b[0:1, sl],
                                                                r_b[0:1, sl], EPS),
                 waits=[(g["v_row"], vd)], inc=(g["v_row"], 1))
        sa = add("scalar", lambda e, sl=sl: e.activation(r_b[0:1, sl], r_b[0:1, sl],
                                                         AF.Ln),
                 waits=[(g["v_row"], vd)], inc=(g["sc_ln"], 1))
        cn_ch_rstd[c] = add(
            "scalar", lambda e, sl=sl: e.activation(r_b[0:1, sl], r_b[0:1, sl],
                                                    AF.Exp, scale=-0.5),
            waits=[(g["sc_ln"], sa)], inc=(g["sc_ln"], 1))
        cn_ch_mr[c] = add(
            "vector",
            lambda e, sl=sl: e.scalar_tensor_tensor(r_c[0:1, sl], r_a[0:1, sl], -1.0,
                                                    r_b[0:1, sl], ALU.mult, ALU.mult),
            waits=[(g["sc_ln"], cn_ch_rstd[c])], inc=(g["v_row"], 1))
    sc_rstd = cn_ch_rstd[3]
    v_mr = cn_ch_mr[3]

    stps.close()
    psA = per.enter_context(nc.psum_tensor("psA", [128, 2 * 512], F32))
    ps = [None, None] + [per.enter_context(nc.psum_tensor(f"ps{i}", [128, 512], F32))
                         for i in range(2, 8)]

    bc_specs = []
    for c in range(4):
        bc_specs.append((r_b, rB, g["sc_ln"], cn_ch_rstd[c], c))
        bc_specs.append((r_c, mrB, g["v_row"], cn_ch_mr[c], c))
    bc_mm_cn = [0] * 8
    bc_cp_cn = [0] * 8
    cn_bc_ch = [0] * 4
    for j, (src, dst, wsem, wval, c) in enumerate(bc_specs):
        par = j % 2
        bc_mm_cn[j] = add(
            "tensor",
            lambda e, src=src, c=c, par=par: e.matmul(
                psA[:, par * 512:(par + 1) * 512], onesrow_f[:, :],
                src[0:1, c * 512:(c + 1) * 512],
                start=True, stop=True),
            waits=[(wsem, wval), (g["v_init"], v_onesr),
                   (g["v_bc"], bc_cp_cn[j - 2] if j >= 2 else 0)],
            inc=(g["pe_stat"], 1))
        bc_cp_cn[j] = add(
            "vector",
            lambda e, dst=dst, c=c, par=par: e.tensor_copy(
                dst[:, c * 512:(c + 1) * 512], psA[:, par * 512:(par + 1) * 512]),
            waits=[(g["pe_stat"], bc_mm_cn[j])], inc=(g["v_bc"], 1))
        if par == 1:
            cn_bc_ch[c] = bc_cp_cn[j]
    v_bc_all = bc_cp_cn[7]

    cn_appj = {}
    cn_app_ch = {}
    for c in range(4):
        for ht in range(NHT):
            j = c * NHT + ht
            sl = j % 4
            tp = j % 2
            dcn = add(
                "sync",
                lambda e, sl=sl, c=c, ht=ht: e.dma_start(
                    xst[sl][:, :],
                    xT[ht * 128:(ht + 1) * 128, c * 512:(c + 1) * 512]),
                waits=[(g["v_app"], cn_appj.get(j - 4, 0))],
                inc=(g[f"d_xst{sl}"], 16))
            mcn = add(
                "vector",
                lambda e, sl=sl, tp=tp, c=c: e.tensor_mul(
                    t1[tp][:, :], xst[sl][:, :], rB[:, c * 512:(c + 1) * 512]),
                waits=[(g[f"d_xst{sl}"], dcn), (g["v_bc"], cn_bc_ch[c]),
                       (g["v_app"], cn_appj.get(j - 2, 0))],
                inc=(g["v_app"], 1))
            cn_appj[j] = add(
                "vector",
                lambda e, tp=tp, c=c, ht=ht: e.tensor_add(
                    xnT[:, ht * S + c * 512: ht * S + (c + 1) * 512],
                    t1[tp][:, :], mrB[:, c * 512:(c + 1) * 512]),
                waits=[(g["v_app"], mcn)],
                inc=(g["v_app"], 1))
            cn_app_ch[(c, ht)] = cn_appj[j]
    cn_apply_done = cn_app_ch[(3, NHT - 1)]
    apply_done_waits = [(g["v_app"], g["v_app"].count), (g["p_app"], g["p_app"].count)]

    lns.close()
    att1 = ExitStack()
    attnT = att1.enter_context(nc.sbuf_tensor("attnT", [128, NHEAD * S], BF16))
    att2 = ExitStack()
    qT = att2.enter_context(nc.sbuf_tensor("qT", [128, 2 * S], BF16))
    kT = att2.enter_context(nc.sbuf_tensor("kT", [128, 2 * S], BF16))
    pr = att2.enter_context(nc.sbuf_tensor("pr", [128, PR_SLOTS * 512], BF16))
    v_sb = att2.enter_context(nc.sbuf_tensor("v_sb", [128, 16 * 512], BF16))
    recipB = att2.enter_context(nc.sbuf_tensor("recipB", [128, 2 * 512], F32))
    sqs = att2.enter_context(nc.sbuf_tensor("sqs", [128, S // 2], BF16))
    qpool = att2.enter_context(nc.sbuf_tensor("qpool", [128, 512], BF16))
    kpool = att2.enter_context(nc.sbuf_tensor("kpool", [128, 512], BF16))

    cn_wv = [0] * 4
    cn_vmm = [0] * 4
    cn_vevac = [0] * 4
    cn_pv_group = [0] * 4
    cn_wq = [0] * NHEAD
    cn_wk = [0] * NHEAD
    cn_qkg = {}
    cn_qev = [0] * NHEAD
    cn_kev = [0] * NHEAD
    cn_negm = [0] * NHEAD
    cn_scores_done = [0] * NHEAD
    cn_sq_done = [0] * NHEAD
    exp_cn = {}
    mask_cn = {}
    pv_cn = {}
    cn_att_ev = {}
    qk_par_last_evac = [0, 0]
    cn_norm_mm = [0]
    qpool_cn = [0]
    cn_pool = [0]
    cn_nred_last = [0]
    gkt = [0]
    unit_kts = {}
    pv_pending = deque()

    def emit_vproj(grp):
        base_waits = [(g["d_wv"], cn_wv[grp])]
        if grp > 0:
            u_last = grp * 16 - 1
            base_waits += [(g["v_vevac"], cn_vevac[grp - 1]),
                           (g["v_qkevac"], qk_par_last_evac[0]),
                           (g["v_qkevac"], qk_par_last_evac[1]),
                           (g["v_nred"], cn_nred_last[0]),
                           (g["v_negm"], cn_negm[grp * 4]),
                           cn_rl.get(u_last, (g["v_rec"], 0)),
                           (g["v_att"], cn_att_ev.get(u_last, 0))]
        last_mm = [0]
        rounds = [list(range(0, 6)), list(range(6, 12)), list(range(12, 16))]
        evac_bank = {}
        bank_done = {}
        for r, sts in enumerate(rounds):
            for ht in range(NHT):
                for bi, st in enumerate(sts):
                    bank = 2 + bi
                    w = []
                    if ht == 0:
                        if r == 0 and st == sts[0]:
                            w = list(base_waits)
                        else:
                            w = [(g["v_vevac"], evac_bank.get(bank, 0))]
                    if grp == 0:
                        w.append((g["v_app"], cn_app_ch[(st // 4, ht)]))
                    rr = add(
                        "tensor",
                        lambda e, st=st, ht=ht, bank=bank: e.matmul(
                            ps[bank][:, :],
                            xnT[:, ht * S + st * 128: ht * S + (st + 1) * 128],
                            wv_sb[:, ht * 512:(ht + 1) * 512],
                            start=(ht == 0), stop=(ht == NHT - 1)),
                        waits=w,
                        inc=(g["pe_v"], 1) if ht == NHT - 1 else None)
                    if rr is not None:
                        last_mm[0] = rr
                        bank_done[bank] = rr
            for bi, st in enumerate(sts):
                bank = 2 + bi
                w = [(g["pe_v"], bank_done[bank])]
                if grp > 0:
                    w.append((g["pe_pvs"], cn_pv_group[grp - 1]))
                cn_vevac[grp] = add(
                    "scalar",
                    lambda e, st=st, bank=bank: e.activation(
                        v_sb[:, st * 512:(st + 1) * 512], ps[bank][:, :], AF.Copy),
                    waits=w, inc=(g["v_vevac"], 1))
                evac_bank[bank] = cn_vevac[grp]
        cn_vmm[grp] = last_mm[0]

    def emit_qkproj(i):
        w = [(g["pe_qk"], cn_qkg.get((i - 1, 0, 3), 0))]
        cn_wq[i] = add(
            "gpsimd",
            lambda e, i=i: e.dma_start(wqk_sb[:, 0:2048], wqk[i, :, :]),
            waits=w, inc=(g["d_wq"], 16))
        cn_wk[i] = add(
            "gpsimd",
            lambda e, i=i: e.dma_start(wqk_sb[:, 2048:4096], wqk[16 + i, :, :]),
            waits=[(g["pe_qk"], cn_qkg.get((i - 1, 1, 3), 0))],
            inc=(g["d_wk"], 16))
        par_i = i % 2
        for half in range(2):
            dst = qT if half == 0 else kT
            dsem = g["d_wq"] if half == 0 else g["d_wk"]
            dcn = cn_wq[i] if half == 0 else cn_wk[i]
            for c in range(4):
                gidx = half * 4 + c
                par = gidx % 2
                last = 0
                for ht in range(NHT):
                    w = []
                    if ht == 0:
                        w = [(dsem, dcn),
                             (g["v_qkevac"], qk_par_last_evac[par]),
                             (g["v_nred"], cn_nred_last[0]),
                             (g["v_negm"], cn_negm[i - 1] if i >= 1 else 0),
                             (g["v_vevac"], cn_vevac[
                                 0 if i == 0 else (i // 4 if i % 4 != 0 else i // 4 - 1)]),
                             ] + ([(g["v_app"], cn_app_ch[(c, NHT - 1)])]
                                  if i == 0 else [])
                    r = add(
                        "tensor",
                        lambda e, half=half, c=c, ht=ht, par=par: e.matmul(
                            ps[6 + par][:, :],
                            wqk_sb[:, half * 2048 + ht * 128: half * 2048 + (ht + 1) * 128],
                            xnT[:, ht * S + c * 512: ht * S + (c + 1) * 512],
                            start=(ht == 0), stop=(ht == NHT - 1)),
                        waits=w,
                        inc=(g["pe_qk"], 1) if ht == NHT - 1 else None)
                    if r is not None:
                        last = r
                cn_qkg[(i, half, c)] = last
                wv_ = [(g["pe_qk"], last)]
                if i >= 2:
                    wv_.append((g["pe_sc"], cn_scores_done[i - 2]))
                    wv_.append((g["v_pool"], cn_sq_done[i - 2]))
                ev = add(
                    "vector",
                    lambda e, dst=dst, c=c, par=par, par_i=par_i: e.tensor_copy(
                        dst[:, par_i * S + c * 512: par_i * S + (c + 1) * 512],
                        ps[6 + par][:, :]),
                    waits=wv_, inc=(g["v_qkevac"], 1))
                qk_par_last_evac[par] = ev
                if half == 0:
                    cn_qev[i] = ev
                else:
                    cn_kev[i] = ev

    def _half_norm(src_t, par_i, pool, first_waits):
        last = None
        for h in range(2):
            w = list(first_waits) if h == 0 else [(g["v_pool"], last)]
            sq = add(
                "vector",
                lambda e, par_i=par_i, h=h, src_t=src_t: e.tensor_mul(
                    sqs[:, :],
                    src_t[:, par_i * S + h * 1024: par_i * S + (h + 1) * 1024],
                    src_t[:, par_i * S + h * 1024: par_i * S + (h + 1) * 1024]),
                waits=w, inc=(g["v_pool"], 1))
            p0 = add(
                "vector",
                lambda e, h=h, pool=pool: e.tensor_max(
                    pool[:, h * 256:(h + 1) * 256],
                    sqs[:, 0:1024:4], sqs[:, 1:1024:4]),
                waits=[(g["v_pool"], sq)], inc=(g["v_pool"], 1))
            p1 = add(
                "vector",
                lambda e, h=h, pool=pool: e.tensor_max(
                    pool[:, h * 256:(h + 1) * 256],
                    pool[:, h * 256:(h + 1) * 256], sqs[:, 2:1024:4]),
                waits=[(g["v_pool"], p0)], inc=(g["v_pool"], 1))
            last = add(
                "vector",
                lambda e, h=h, pool=pool: e.tensor_max(
                    pool[:, h * 256:(h + 1) * 256],
                    pool[:, h * 256:(h + 1) * 256], sqs[:, 3:1024:4]),
                waits=[(g["v_pool"], p1)], inc=(g["v_pool"], 1))
        return last

    def emit_norms_q(i):
        par_i = i % 2
        cn_pool[0] = _half_norm(
            qT, par_i, qpool,
            [(g["v_qkevac"], cn_qev[i]), (g["pe_norm"], cn_norm_mm[0]),
             (g["v_pool"], cn_pool[0])])
        qpool_cn[0] = cn_pool[0]
        cn_pool[0] = _half_norm(
            kT, par_i, kpool,
            [(g["v_qkevac"], cn_kev[i]), (g["v_pool"], cn_pool[0]),
             (g["pe_norm"], cn_norm_mm[0])])
        cn_sq_done[i] = cn_pool[0]

    def emit_norms_k(i):
        mm = add(
            "tensor",
            lambda e: e.matmul(ps[7][0:1, :], ones_bf[:, 0:1], qpool[:, :],
                               start=True, stop=True),
            waits=[(g["v_pool"], qpool_cn[0]), (g["v_nred"], cn_nred_last[0]),
                   (g["v_qkevac"], qk_par_last_evac[1]),
                   (g["v_negm"], cn_negm[i - 1] if i >= 1 else 0)],
            inc=(g["pe_norm"], 1))
        cn_norm_mm[0] = mm
        cn_nred_last[0] = add(
            "vector",
            lambda e: e.reduce_max(nm_sc[0:1, 0:1], ps[7][0:1, :], axis=AX.X),
            waits=[(g["pe_norm"], mm), (g["v_nred"], cn_nred_last[0])],
            inc=(g["v_nred"], 1))
        mm = add(
            "tensor",
            lambda e: e.matmul(ps[7][0:1, :], ones_bf[:, 0:1], kpool[:, :],
                               start=True, stop=True),
            waits=[(g["v_pool"], cn_pool[0]), (g["v_nred"], cn_nred_last[0])],
            inc=(g["pe_norm"], 1))
        cn_norm_mm[0] = mm
        nred = add(
            "vector",
            lambda e: e.reduce_max(nm_sc[0:1, 1:2], ps[7][0:1, :], axis=AX.X),
            waits=[(g["pe_norm"], mm), (g["v_nred"], cn_nred_last[0])],
            inc=(g["v_nred"], 1))
        cn_nred_last[0] = add(
            "vector",
            lambda e: e.tensor_add(nm_sc[0:1, 10:11], nm_sc[0:1, 0:1],
                                   nm_sc[0:1, 1:2]),
            waits=[(g["v_nred"], nred)], inc=(g["v_nred"], 1))
        mm = add("tensor",
                 lambda e: e.matmul(ps[7][:, 0:1], onesrow_f[:, :],
                                    nm_sc[0:1, 10:11], start=True, stop=True),
                 waits=[(g["v_nred"], cn_nred_last[0])],
                 inc=(g["pe_norm"], 1))
        cn_norm_mm[0] = mm
        cn_negm[i] = add(
            "vector",
            lambda e, i=i: e.tensor_scalar_mul(negM[:, i:i + 1], ps[7][:, 0:1],
                                               -SCALE / 2.0),
            waits=[(g["pe_norm"], mm)], inc=(g["v_negm"], 1))


    cn_rl = {}

    def pump_pv_one():
        st = pv_pending[0]
        i, qc, uidx = st["i"], st["qc"], st["uidx"]
        kts = st["kts"]
        n = st["n"]
        kt, gi, off, width, slot = kts[n]
        L = st["L"]
        sb = 2 + 2 * (uidx % 2)
        ab = 3 + 2 * (uidx % 2)
        wkt = [(g["sc_exp"], exp_cn[gi])]
        if gi in mask_cn:
            wkt.append((g["v_mask"], mask_cn[gi]))
        if n == 0:
            wkt.append(cn_rl.get(uidx - 2, (g["v_rec"], 0)))
            if uidx % 16 < 2:
                wkt.append((g["v_vevac"], cn_vevac[uidx // 16]))
        add("tensor",
            lambda e, off=off, width=width, slot=slot, n=n, L=L, sb=sb: e.matmul(
                ps[sb][:, off:512], ones_bf[:, :],
                pr[:, slot * 512: slot * 512 + width],
                start=(n == 0), stop=(n == L - 1)),
            waits=wkt)
        hd_loc = i % 4
        wpv = [(g["v_att"], cn_att_ev.get(uidx - 2, 0))] if n == 0 else []
        if n == 0 and uidx % 16 < 2:
            wpv.append((g["v_vevac"], cn_vevac[uidx // 16]))
        pv_cn[gi] = add(
            "tensor",
            lambda e, kt=kt, hd_loc=hd_loc, off=off, width=width, slot=slot,
            n=n, L=L, ab=ab: e.matmul(
                ps[ab][:, off:512],
                v_sb[:, kt * 512 + hd_loc * 128: kt * 512 + (hd_loc + 1) * 128],
                pr[:, slot * 512: slot * 512 + width],
                start=(n == 0), stop=(n == L - 1)),
            waits=wpv,
            inc=(g["pe_pvs"], 1))
        st["n"] += 1
        if st["n"] == L:
            upar = uidx % 2
            if uidx < 0:
                rcl = add("scalar",
                          lambda e, upar=upar, sb=sb: e.activation(
                              recipB[:, upar * 512:(upar + 1) * 512], ps[sb][:, :],
                              AF.Ln),
                          waits=[(g["pe_pvs"], pv_cn[gi]),
                                 (g["v_att"], cn_att_ev.get(uidx - 2, 0))],
                          inc=(g["sc_rec"], 1))
                rc_ = add("scalar",
                          lambda e, upar=upar: e.activation(
                              recipB[:, upar * 512:(upar + 1) * 512],
                              recipB[:, upar * 512:(upar + 1) * 512],
                              AF.Exp, scale=-1.0),
                          waits=[(g["sc_rec"], rcl)],
                          inc=(g["sc_rec"], 1))
                cn_rl[uidx] = (g["sc_rec"], rc_)
            else:
                rc_ = add("vector",
                          lambda e, upar=upar, sb=sb: e.reciprocal(
                              recipB[:, upar * 512:(upar + 1) * 512], ps[sb][:, :]),
                          waits=[(g["pe_pvs"], pv_cn[gi]),
                                 (g["v_att"], cn_att_ev.get(uidx - 2, 0))],
                          inc=(g["v_rec"], 1))
                cn_rl[uidx] = (g["v_rec"], rc_)
            cn_att_ev[uidx] = add(
                "vector",
                lambda e, i=i, qc=qc, upar=upar, ab=ab: e.tensor_tensor(
                    attnT[:, i * S + qc * 512: i * S + (qc + 1) * 512],
                    ps[ab][:, :], recipB[:, upar * 512:(upar + 1) * 512], ALU.mult),
                waits=[cn_rl[uidx]],
                inc=(g["v_att"], 1))
            pv_pending.popleft()

    def pump_all():
        while pv_pending:
            pump_pv_one()

    def emit_scores(i, qc):
        par_i = i % 2
        nkt = 4 * qc + 4
        st_rec = {"i": i, "qc": qc, "uidx": i * 4 + qc, "kts": [], "n": 0, "L": nkt}
        pv_pending.append(st_rec)
        kts = st_rec["kts"]
        for kt in range(nkt):
            gi = gkt[0]
            gkt[0] += 1
            off = max(0, kt * 128 - qc * 512)
            width = 512 - off
            slot = gi % PR_SLOTS
            par = gi % 2
            while gi >= PR_SLOTS and (gi - PR_SLOTS) not in pv_cn:
                pump_pv_one()
            w = [(g["v_qkevac"], max(cn_qev[i], cn_kev[i]))] if kt == 0 else []
            if gi >= 2:
                w.append((g["sc_exp"], exp_cn[gi - 2]))
            cn = add(
                "tensor",
                lambda e, kt=kt, off=off, par=par, par_i=par_i, qc=qc: e.matmul(
                    psA[:, par * 512 + off: par * 512 + 512],
                    kT[:, par_i * S + kt * 128: par_i * S + (kt + 1) * 128],
                    qT[:, par_i * S + qc * 512 + off: par_i * S + (qc + 1) * 512],
                    start=True, stop=True),
                waits=w, inc=(g["pe_sc"], 1))
            cn_scores_done[i] = cn
            we = [(g["pe_sc"], cn)]
            if kt == 0:
                we.append((g["v_negm"], cn_negm[i]))
            if gi >= PR_SLOTS:
                we.append((g["pe_pvs"], pv_cn[gi - PR_SLOTS]))
            exp_cn[gi] = add(
                "scalar",
                lambda e, slot=slot, width=width, off=off, par=par, i=i: e.activation(
                    pr[:, slot * 512: slot * 512 + width],
                    psA[:, par * 512 + off: par * 512 + 512],
                    AF.Exp, bias=negM[:, i:i + 1], scale=SCALE),
                waits=we, inc=(g["sc_exp"], 1))
            if kt >= 4 * qc:
                mask_cn[gi] = add(
                    "vector",
                    lambda e, slot=slot: e.tensor_mul(
                        pr[:, slot * 512: slot * 512 + 128],
                        pr[:, slot * 512: slot * 512 + 128], mask_bf[:, :]),
                    waits=[(g["sc_exp"], exp_cn[gi]), (g["d_mask"], d_mask_all)],
                    inc=(g["v_mask"], 1))
            kts.append((kt, gi, off, width, slot))
        unit_kts[(i, qc)] = kts

    cn_wv[0] = wv0_state["cast"]
    assert wv0_state["jp"] == 4 and cn_wv[0] == 64
    emit_vproj(0)
    emit_qkproj(0)
    emit_norms_q(0)
    emit_norms_k(0)
    for i in range(NHEAD):
        grp = i // 4
        for qc in range(4):
            if qc == 3 and i + 1 < NHEAD:
                emit_norms_q(i + 1)
            emit_scores(i, qc)
            if qc == 2 and i + 1 < NHEAD:
                emit_qkproj(i + 1)
        if i + 1 < NHEAD:
            emit_norms_k(i + 1)
        if i % 4 == 3:
            pump_all()
            cn_pv_group[grp] = g["pe_pvs"].count
            if grp + 1 < 4:
                cn_wv[grp + 1] = add(
                    "gpsimd",
                    lambda e, grp=grp: e.dma_start(wv_sb[:, :], wv4[grp + 1, :, :]),
                    waits=[(g["pe_v"], cn_vmm[grp])],
                    inc=(g["d_wv"], 16))
                emit_vproj(grp + 1)
    pump_all()
    cn_att_final = g["v_att"].count
    cn_rec_final = g["v_rec"].count
    cn_screc_final = g["sc_rec"].count
    cn_exp_final = g["sc_exp"].count
    cn_sc_final = g["pe_sc"].count
    cn_pvs_final = g["pe_pvs"].count
    cn_nsq_final = g["v_pool"].count
    cn_vevac_final = g["v_vevac"].count

    att2.close()
    wos = ExitStack()
    sT_st = [wos.enter_context(nc.sbuf_tensor(f"sT{i}", [128, S], BF16)) for i in range(2)]

    cn_wod = [0] * 16
    cn_wog = {}
    cn_sTev = [0] * 16
    cn_spill = [0] * 16
    wo_par_last_evac = [0, 0]
    first_sT = True
    for ot in range(16):
        w = [(g["pe_qk"], cn_qkg[(NHEAD - 1, 1, 3)])] if ot < 2 else \
            [(g["pe_wo"], cn_wog[(ot - 2, 3)])]
        cn_wod[ot] = add(
            "gpsimd",
            lambda e, ot=ot: e.dma_start(wqk_sb[:, (ot % 2) * 2048:(ot % 2 + 1) * 2048],
                                         wo[ot, :, :]),
            waits=w, inc=(g[f"d_wo{ot % 2}"], 16))
        for c in range(4):
            par = (ot * 4 + c) % 2
            last = 0
            for ht in range(NHT):
                w = []
                if ht == 0:
                    w = [(g[f"d_wo{ot % 2}"], cn_wod[ot]), (g["v_att"], cn_att_final),
                         (g["v_sT"], wo_par_last_evac[par]),
                         (g["v_negm"], cn_negm[NHEAD - 1]),
                         (g["v_nred"], cn_nred_last[0]),
                         (g["v_rec"], cn_rec_final),
                         (g["sc_rec"], cn_screc_final)]
                r = add(
                    "tensor",
                    lambda e, ot=ot, c=c, ht=ht, par=par: e.matmul(
                        ps[4 + par][:, :],
                        wqk_sb[:, (ot % 2) * 2048 + ht * 128: (ot % 2) * 2048 + (ht + 1) * 128],
                        attnT[:, ht * S + c * 512: ht * S + (c + 1) * 512],
                        start=(ht == 0), stop=(ht == NHT - 1)),
                    waits=w,
                    inc=(g["pe_wo"], 1) if ht == NHT - 1 else None)
                if r is not None:
                    last = r
            cn_wog[(ot, c)] = last
            w = [(g["pe_wo"], last)]
            if first_sT:
                w += [(g["pe_sc"], cn_sc_final), (g["v_pool"], cn_nsq_final),
                      (g["pe_pvs"], cn_pvs_final), (g["sc_exp"], cn_exp_final)]
                first_sT = False
            if ot >= 2:
                w.append((g[f"d_sp{ot % 2}"], cn_spill[ot - 2]))
            ev = add(
                "vector",
                lambda e, ot=ot, c=c, par=par: e.tensor_copy(
                    sT_st[ot % 2][:, c * 512:(c + 1) * 512], ps[4 + par][:, :]),
                waits=w, inc=(g["v_sT"], 1))
            wo_par_last_evac[par] = ev
            cn_sTev[ot] = ev
        cn_spill[ot] = add(
            "sync",
            lambda e, ot=ot: e.dma_start(spill[ot * 128:(ot + 1) * 128, :],
                                         sT_st[ot % 2][:, :]),
            waits=[(g["v_sT"], cn_sTev[ot])], inc=(g[f"d_sp{ot % 2}"], 16))
    cn_wo_final = g["pe_wo"].count
    cn_vsT_final = g["v_sT"].count
    cn_spill_final = cn_spill[15]

    wos.close()
    att1.close()
    ffs = ExitStack()
    wg_sb = ffs.enter_context(nc.sbuf_tensor("wg_sb", [128, 2 * 2048], BF16))
    wu_sb = ffs.enter_context(nc.sbuf_tensor("wu_sb", [128, 2 * 2048], BF16))
    wd_sb = ffs.enter_context(nc.sbuf_tensor("wd_sb", [128, 2 * NFT * 128], BF16))
    y_sb = ffs.enter_context(nc.sbuf_tensor("y_sb", [128, NFT * 512], BF16))
    tmpg = ffs.enter_context(nc.sbuf_tensor("tmpg", [128, 2 * 512], BF16))
    tmpu = ffs.enter_context(nc.sbuf_tensor("tmpu", [128, 2 * 512], BF16))
    gf = ffs.enter_context(nc.sbuf_tensor("gf", [128, 2 * 512], F32))
    xf = ffs.enter_context(nc.sbuf_tensor("xf", [128, 2 * 512], F32))
    sf = ffs.enter_context(nc.sbuf_tensor("sf", [128, 2 * 512], F32))
    of = ffs.enter_context(nc.sbuf_tensor("of", [128, 2 * 512], F32))

    coarse_ffn = [(g["pe_wo"], cn_wo_final), (g["v_sT"], cn_vsT_final),
                  (g["d_sp0"], cn_spill[14]), (g["d_sp1"], cn_spill[15]),
                  (g["pe_pvs"], cn_pvs_final), (g["v_rec"], cn_rec_final),
                  (g["sc_rec"], cn_screc_final),
                  (g["sc_exp"], cn_exp_final), (g["v_vevac"], cn_vevac_final)]

    def prev2(d, sc, k, percol):
        if k >= 2:
            return d.get((sc, k - 2), 0)
        if sc > 0:
            return d.get((sc - 1, percol - 2 + k), 0)
        return 0

    cn_wgd = {}
    cn_wud = {}
    cn_wdd = {}
    cn_gug = {}
    cn_uug = {}
    cn_sg_ev = {}
    cn_su_ev = {}
    cn_y = {}
    cn_ny = [0] * 4
    cn_dng = {}
    cn_gsil = {}
    cn_fin = {}
    cn_xfd = {}
    cn_sfd = {}
    cn_outd = {}

    for sc in range(4):
        for ht in range(NFT):
            hp = ht % 2
            w = [(g["pe_gu"], prev2(cn_gug, sc, ht, NFT))]
            if sc == 0 and ht < 2:
                w = list(coarse_ffn)
            cn_wgd[(sc, ht)] = add(
                "gpsimd",
                lambda e, ht=ht, hp=hp: e.dma_start(
                    wg_sb[:, hp * 2048:(hp + 1) * 2048], wg[ht, :, :]),
                waits=w, inc=(g[f"d_wg{hp}"], 16))
            w = [(g["pe_gu"], prev2(cn_uug, sc, ht, NFT))]
            if sc == 0 and ht < 2:
                w = list(coarse_ffn)
            cn_wud[(sc, ht)] = add(
                "gpsimd",
                lambda e, ht=ht, hp=hp: e.dma_start(
                    wu_sb[:, hp * 2048:(hp + 1) * 2048], wu[ht, :, :]),
                waits=w, inc=(g[f"d_wu{hp}"], 16))
            for dt in range(NHT):
                w = []
                if dt == 0:
                    w = [(g[f"d_wg{hp}"], cn_wgd[(sc, ht)]),
                         (g["sc_gu"], prev2(cn_sg_ev, sc, ht, NFT))]
                r = add("tensor",
                        lambda e, ht=ht, dt=dt, hp=hp, sc=sc: e.matmul(
                            psA[:, hp * 512:(hp + 1) * 512],
                            wg_sb[:, hp * 2048 + dt * 128: hp * 2048 + (dt + 1) * 128],
                            xnT[:, dt * S + sc * 512: dt * S + (sc + 1) * 512],
                            start=(dt == 0), stop=(dt == NHT - 1)),
                        waits=w,
                        inc=(g["pe_gu"], 1) if dt == NHT - 1 else None)
            cn_gug[(sc, ht)] = r
            for dt in range(NHT):
                w = []
                if dt == 0:
                    w = [(g[f"d_wu{hp}"], cn_wud[(sc, ht)]),
                         (g["sc_gu"], prev2(cn_su_ev, sc, ht, NFT))]
                r = add("tensor",
                        lambda e, ht=ht, dt=dt, hp=hp, sc=sc: e.matmul(
                            ps[2 + hp][:, :],
                            wu_sb[:, hp * 2048 + dt * 128: hp * 2048 + (dt + 1) * 128],
                            xnT[:, dt * S + sc * 512: dt * S + (sc + 1) * 512],
                            start=(dt == 0), stop=(dt == NHT - 1)),
                        waits=w,
                        inc=(g["pe_gu"], 1) if dt == NHT - 1 else None)
            cn_uug[(sc, ht)] = r
            cn_sg_ev[(sc, ht)] = add(
                "scalar",
                lambda e, hp=hp, ht=ht: e.activation(
                    tmpg[:, hp * 512:(hp + 1) * 512],
                    psA[:, hp * 512:(hp + 1) * 512], AF.Silu,
                    scale=sg_sb[:, ht:ht + 1]),
                waits=[(g["pe_gu"], cn_gug[(sc, ht)]), (g["d_scl"], d_scl_all),
                       (g["v_y"], prev2(cn_y, sc, ht, NFT))],
                inc=(g["sc_gu"], 1))
            cn_su_ev[(sc, ht)] = add(
                "scalar",
                lambda e, hp=hp, ht=ht: e.activation(
                    tmpu[:, hp * 512:(hp + 1) * 512], ps[2 + hp][:, :], AF.Copy,
                    scale=su_sb[:, ht:ht + 1]),
                waits=[(g["pe_gu"], cn_uug[(sc, ht)]),
                       (g["v_y"], prev2(cn_y, sc, ht, NFT))],
                inc=(g["sc_gu"], 1))
            w = [(g["sc_gu"], cn_su_ev[(sc, ht)])]
            if ht == 0 and sc > 0:
                w.append((g["pe_dn"], cn_dng[(sc - 1, 15)]))
            cn_y[(sc, ht)] = add(
                "vector",
                lambda e, hp=hp, ht=ht: e.tensor_mul(
                    y_sb[:, ht * 512:(ht + 1) * 512],
                    tmpg[:, hp * 512:(hp + 1) * 512],
                    tmpu[:, hp * 512:(hp + 1) * 512]),
                waits=w, inc=(g["v_y"], 1))
        for dt in range(16):
            dp = dt % 2
            w = [(g["pe_dn"], prev2(cn_dng, sc, dt, 16))]
            if sc == 0 and dt < 2:
                w = list(coarse_ffn)
            cn_wdd[(sc, dt)] = add(
                "gpsimd",
                lambda e, dt=dt, dp=dp: e.dma_start(
                    wd_sb[:, dp * NFT * 128:(dp + 1) * NFT * 128], wdn[dt, :, :]),
                waits=w, inc=(g[f"d_wd{dp}"], 16))
            cn_xfd[(sc, dt)] = add(
                "sync",
                lambda e, sc=sc, dt=dt, dp=dp: e.dma_start(
                    xf[:, dp * 512:(dp + 1) * 512],
                    xT[dt * 128:(dt + 1) * 128, sc * 512:(sc + 1) * 512]),
                waits=[(g["v_fin"], prev2(cn_fin, sc, dt, 16))],
                inc=(g[f"d_xf{dp}"], 16))
            cn_sfd[(sc, dt)] = add(
                "gpsimd",
                lambda e, sc=sc, dt=dt, dp=dp: e.dma_start(
                    sf[:, dp * 512:(dp + 1) * 512],
                    spill[dt * 128:(dt + 1) * 128, sc * 512:(sc + 1) * 512]),
                waits=[(g[f"d_sp{dt % 2}"], cn_spill[dt]),
                       (g["v_fin"], prev2(cn_fin, sc, dt, 16))],
                inc=(g[f"d_sf{dp}"], 16))
            for ht in range(NFT):
                w = []
                if ht == 0:
                    w = [(g[f"d_wd{dp}"], cn_wdd[(sc, dt)]),
                         (g["v_y"], cn_y[(sc, NFT - 1)]),
                         (g["sc_dn"], prev2(cn_gsil, sc, dt, 16))]
                r = add("tensor",
                        lambda e, dt=dt, ht=ht, dp=dp: e.matmul(
                            ps[4 + dp][:, :],
                            wd_sb[:, dp * NFT * 128 + ht * 128: dp * NFT * 128 + (ht + 1) * 128],
                            y_sb[:, ht * 512:(ht + 1) * 512],
                            start=(ht == 0), stop=(ht == NFT - 1)),
                        waits=w,
                        inc=(g["pe_dn"], 1) if ht == NFT - 1 else None)
            cn_dng[(sc, dt)] = r
            cn_gsil[(sc, dt)] = add(
                "scalar",
                lambda e, dp=dp, dt=dt: e.activation(
                    gf[:, dp * 512:(dp + 1) * 512], ps[4 + dp][:, :], AF.Silu,
                    scale=sd_sb[:, dt:dt + 1]),
                waits=[(g["pe_dn"], cn_dng[(sc, dt)]),
                       (g["v_fin"], prev2(cn_fin, sc, dt, 16))],
                inc=(g["sc_dn"], 1))
            fmul = add("vector",
                       lambda e, dp=dp: e.tensor_mul(of[:, dp * 512:(dp + 1) * 512],
                                                     gf[:, dp * 512:(dp + 1) * 512],
                                                     sf[:, dp * 512:(dp + 1) * 512]),
                       waits=[(g["sc_dn"], cn_gsil[(sc, dt)]),
                              (g[f"d_sf{dp}"], cn_sfd[(sc, dt)]),
                              (g[f"d_ou{dp}"], prev2(cn_outd, sc, dt, 16))],
                       inc=(g["v_fin"], 1))
            cn_fin[(sc, dt)] = add(
                "vector",
                lambda e, dp=dp: e.tensor_add(of[:, dp * 512:(dp + 1) * 512],
                                              of[:, dp * 512:(dp + 1) * 512],
                                              xf[:, dp * 512:(dp + 1) * 512]),
                waits=[(g[f"d_xf{dp}"], cn_xfd[(sc, dt)]), (g["v_fin"], fmul)],
                inc=(g["v_fin"], 1))
            cn_outd[(sc, dt)] = add(
                "sync",
                lambda e, sc=sc, dt=dt, dp=dp: e.dma_start(
                    outT[dt * 128:(dt + 1) * 128, sc * 512:(sc + 1) * 512],
                    of[:, dp * 512:(dp + 1) * 512]),
                waits=[(g["v_fin"], cn_fin[(sc, dt)])],
                inc=(g[f"d_ou{dp}"], 16))

    add("sync", None, waits=[(g["d_ou0"], 32 * 16), (g["d_ou1"], 32 * 16)])

    with nc.Block() as block:
        @block.sync
        def _(e):
            sch.emit("sync", e)

        @block.gpsimd
        def _(e):
            sch.emit("gpsimd", e)

        @block.tensor
        def _(e):
            sch.emit("tensor", e)

        @block.scalar
        def _(e):
            sch.emit("scalar", e)

        @block.vector
        def _(e):
            sch.emit("vector", e)

    ffs.close()
    per.close()
    return nc


_NC_CACHE = {}


def _get_nc():
    if "nc" not in _NC_CACHE:
        _NC_CACHE["nc"] = build_nc()
    return _NC_CACHE["nc"]


def _prep_weights(w_qkv, w_o, wq_gate, s_gate, wq_up, s_up, wq_down, s_down):
    f32 = np.float32
    w_qkv = np.asarray(w_qkv, dtype=f32)
    w_o = np.asarray(w_o, dtype=f32)
    wqk = np.ascontiguousarray(
        w_qkv[:4096].reshape(32, 128, 16, 128).transpose(0, 3, 2, 1)
    ).reshape(32, 128, 2048)
    wv4 = np.ascontiguousarray(
        w_qkv[4096:].reshape(4, 512, 16, 128).transpose(0, 3, 2, 1)
    ).reshape(4, 128, 16 * 512)
    wo_t = np.ascontiguousarray(
        w_o.reshape(16, 128, 16, 128).transpose(0, 3, 2, 1)
    ).reshape(16, 128, 2048)
    def i8zp(a):
        return (np.asarray(a).astype(np.uint8) ^ np.uint8(0x80)).view(np.int8)
    wg_t = np.ascontiguousarray(
        i8zp(wq_gate).reshape(44, 128, 16, 128).transpose(0, 3, 2, 1)
    ).reshape(44, 128, 2048)
    wu_t = np.ascontiguousarray(
        i8zp(wq_up).reshape(44, 128, 16, 128).transpose(0, 3, 2, 1)
    ).reshape(44, 128, 2048)
    wd_t = np.ascontiguousarray(
        i8zp(wq_down).reshape(16, 128, 44, 128).transpose(0, 3, 2, 1)
    ).reshape(16, 128, 44 * 128)
    sg_t = np.ascontiguousarray(np.asarray(s_gate, dtype=f32).reshape(44, 128).T)
    su_t = np.ascontiguousarray(np.asarray(s_up, dtype=f32).reshape(44, 128).T)
    sd_t = np.ascontiguousarray(np.asarray(s_down, dtype=f32).reshape(16, 128).T)
    mask = np.ascontiguousarray(np.triu(np.ones((128, 128), dtype=f32)))
    return dict(wqk=wqk, wv4=wv4, wo=wo_t, wg=wg_t, wu=wu_t, wdn=wd_t,
                sg=sg_t, su=su_t, sd=sd_t, mask_f=mask)


def kernel(L0_attn_hidden_states, L0_attn_w_qkv, L0_attn_w_o,
           L0_ffn_wq_gate, L0_ffn_s_gate, L0_ffn_wq_up, L0_ffn_s_up,
           L0_ffn_wq_down, L0_ffn_s_down):
    x = np.asarray(L0_attn_hidden_states, dtype=np.float32)
    B = x.shape[0]
    shared = _prep_weights(L0_attn_w_qkv, L0_attn_w_o,
                           L0_ffn_wq_gate, L0_ffn_s_gate,
                           L0_ffn_wq_up, L0_ffn_s_up,
                           L0_ffn_wq_down, L0_ffn_s_down)
    in_maps = []
    for b in range(B):
        m = dict(shared)
        m["xT"] = np.ascontiguousarray(x[b].T)
        in_maps.append(m)
    nc = _get_nc()
    res = run_bass_kernel_spmd(nc, in_maps, core_ids=list(range(B)))
    out = np.empty((B, S, H), dtype=np.float32)
    for b in range(B):
        out[b] = np.asarray(res.results[b]["outT"]).T
    return out

